# version 12
# baseline (speedup 1.0000x reference)
import numpy as np
import concourse.bass as bass
import concourse.mybir as mybir
from concourse.bass_utils import run_bass_kernel_spmd

F32 = mybir.dt.float32
BF16 = mybir.dt.bfloat16
AF = mybir.ActivationFunctionType
ALU = mybir.AluOpType
AX = mybir.AxisListType

D = 2048
T = 2048
NCORE = 8
NG = 4
KC = 16
NB = 4
FC = 11
NSH = 2
NE = 18
HF = 2
TH = T // HF
MT = TH // 128
EPS = 1e-5
NEG = -1.0e30
EORD = list(range(NE))
DBG = False


def _t(a):
    return np.ascontiguousarray(a)


def host_prepare(i):
    d = {}
    for nm, w in (("wq", i["L0_attn_wq_w"]), ("wk", i["L0_attn_wk_w"])):
        d[nm] = _t(np.asarray(w, np.float32).reshape(16, 128, 16, 128)
                   .transpose(0, 3, 2, 1).reshape(16, 128, 2048))
    d["wv"] = _t(np.asarray(i["L0_attn_wv_w"], np.float32).T.reshape(16, 128, 2048))
    d["wo"] = _t(np.asarray(i["L0_attn_wo_w"], np.float32).T.reshape(16, 128, 2048))
    qb = np.asarray(i["L0_attn_wq_b"], np.float32).reshape(16, 128).T
    kb = np.asarray(i["L0_attn_wk_b"], np.float32).reshape(16, 128).T
    d["qkb"] = _t(np.concatenate([qb, kb], axis=1))
    d["wvb"] = _t(np.broadcast_to(np.asarray(i["L0_attn_wv_b"], np.float32), (128, 2048)))
    d["wob"] = _t(np.broadcast_to(np.asarray(i["L0_attn_wo_b"], np.float32), (128, 2048)))
    w1 = np.concatenate([np.asarray(i["L0_ffn_shared_w1"], np.float32),
                         np.asarray(i["L0_ffn_routed_w1"], np.float32)], axis=0)
    w2 = np.concatenate([np.asarray(i["L0_ffn_shared_w2"], np.float32),
                         np.asarray(i["L0_ffn_routed_w2"], np.float32)], axis=0)
    b1 = np.concatenate([np.asarray(i["L0_ffn_shared_b1"], np.float32),
                         np.asarray(i["L0_ffn_routed_b1"], np.float32)], axis=0)
    b2 = np.concatenate([np.asarray(i["L0_ffn_shared_b2"], np.float32),
                         np.asarray(i["L0_ffn_routed_b2"], np.float32)], axis=0)
    d["w1"] = _t(w1.reshape(NE, 16, 128, 11, 128).transpose(0, 3, 2, 1, 4)
                 .reshape(NE, 11, 128, 2048))
    d["w2"] = _t(w2.reshape(NE, 11, 128, 2048))
    d["b1"] = _t(b1.reshape(NE, 11, 128).transpose(2, 0, 1).reshape(128, NE * 11))
    d["b2"] = _t(b2)
    d["cent"] = _t(np.asarray(i["L0_ffn_centroids"], np.float32).T
                   .reshape(16, 128, 16).transpose(1, 0, 2).reshape(128, 256))
    d["cbb"] = _t(np.broadcast_to(np.asarray(i["L0_ffn_bias"], np.float32), (128, 16)))
    pos = np.arange(T, dtype=np.float64)
    fr = np.arange(32, dtype=np.float64) / 32.0
    ang = pos[None, :] / (10000.0 ** fr)[:, None]
    ch = np.cos(ang).astype(np.float32)
    sh = np.sin(ang).astype(np.float32)
    rc = np.zeros((16, 128, T), np.float32)
    rs = np.zeros((16, 128, T), np.float32)
    for c in range(16):
        rc[c, 0:64] = ch[2 * c]
        rc[c, 64:128] = ch[2 * c + 1]
        rs[c, 0:32] = sh[2 * c]
        rs[c, 32:64] = -sh[2 * c]
        rs[c, 64:96] = sh[2 * c + 1]
        rs[c, 96:128] = -sh[2 * c + 1]
    d["ropeC"] = _t(rc)
    d["ropeS"] = _t(rs)
    tri = np.where(np.arange(128)[:, None] <= np.arange(128)[None, :], 0.0, NEG)
    d["tri"] = _t(tri.astype(np.float32))
    d["iden"] = _t(np.eye(128, dtype=np.float32))
    d["onesr"] = _t(np.ones((1, 128), np.float32))
    d["onesg"] = _t(np.ones((2, 1024), np.float32))
    o2 = np.zeros((128, 2), np.float32)
    o2[0:64, 0] = 1.0
    o2[64:128, 1] = 1.0
    d["ones2"] = _t(o2)
    return d


AL = 32


class Cur:
    def __init__(self, start):
        self.o = (start + AL - 1) // AL * AL
        self.end = self.o

    def take(self, nbytes):
        o = self.o
        self.o = (o + nbytes + AL - 1) // AL * AL
        self.end = max(self.end, self.o)
        return o


def plan_layout():
    L = {}

    def put(cur, name, shape, dtype, at=None):
        nb = int(np.prod(shape[1:])) * mybir.dt.size(dtype)
        off = cur.take(nb) if at is None else at
        L[name] = (shape, dtype, off)

    perm = Cur(0)
    put(perm, "idenf", [128, 128], F32)
    put(perm, "idenb", [128, 128], BF16)
    put(perm, "onesb", [1, 128], BF16)
    put(perm, "onesf", [1, 128], F32)
    put(perm, "ones2b", [128, 2], BF16)
    put(perm, "rows", [128, 96], F32)
    put(perm, "epsc", [128, 1], F32)
    put(perm, "qrow", [1, 32], F32)
    put(perm, "krow", [1, 32], F32)
    put(perm, "crow", [1, 32], F32)
    put(perm, "rcpA", [1, 512], BF16)
    put(perm, "rcpB", [1, 512], BF16)
    PERM_END = perm.end

    s1 = Cur(PERM_END)
    put(s1, "cosT0", [128, 2048], BF16)
    put(s1, "cosT1", [128, 2048], BF16)
    put(s1, "sinT0", [128, 2048], BF16)
    put(s1, "sinT1", [128, 2048], BF16)
    put(s1, "trib", [128, 128], BF16)
    put(s1, "qkb", [128, 32], F32)
    put(s1, "expb", [128, 32], F32)
    s3 = Cur(PERM_END)
    put(s3, "b2sb", [18, 2048], BF16)
    put(s3, "centT", [128, 256], BF16)
    put(s3, "b1h", [128, NE * 11], F32)
    put(s3, "cbb", [128, 16], F32)
    put(s3, "gT", [18, 1024], BF16)
    put(s3, "g_all", [128, 128], F32)
    put(s3, "scs", [128, 16], F32)
    put(s3, "msk", [128, 16], F32)
    put(s3, "gt16", [128, 16], F32)
    SMALL_END = max(s1.end, s3.end)

    b1c = Cur(SMALL_END)
    put(b1c, "xhat", [128, 16 * 2048], BF16)
    P2_START = b1c.o
    put(b1c, "Q", [128, 4 * 2048], BF16)
    put(b1c, "K", [128, 4 * 2048], BF16)
    put(b1c, "V", [128, 16 * 520], BF16)
    put(b1c, "stg0", [128, 2048], BF16)
    put(b1c, "stg1", [128, 2048], BF16)
    put(b1c, "t1b", [128, 2048], BF16)
    put(b1c, "t2b", [128, 2048], BF16)
    put(b1c, "wvg", [128, 16 * 512], BF16)
    put(b1c, "wqk0", [128, 2048], BF16)
    put(b1c, "wqk1", [128, 2048], BF16)
    put(b1c, "bvb", [128, 2048], F32)
    put(b1c, "smA", [1, 512], BF16)
    put(b1c, "smB", [1, 512], BF16)
    put(b1c, "smA1", [1, 512], BF16)
    put(b1c, "smB1", [1, 512], BF16)
    put(b1c, "ctstg0", [128, 512], BF16)
    put(b1c, "ctstg1", [128, 512], BF16)
    put(b1c, "ptP0", [128, 1024], BF16)
    put(b1c, "ptP1", [128, 1024], BF16)
    put(b1c, "ptP2", [128, 1024], BF16)
    L["xtf0"] = ([128, 2048], F32, L["Q"][2])
    L["xtf1"] = ([128, 2048], F32, L["Q"][2] + 8192)
    L["xin0"] = ([128, 2048], F32, L["t1b"][2])
    L["xin1"] = ([128, 2048], F32, L["wvg"][2])
    P1_END = b1c.end

    b2c = Cur(P2_START)
    put(b2c, "wosb", [128, 16 * 2048], BF16, at=L["xhat"][2])
    put(b2c, "ctx", [128, 16 * 2048], BF16)
    put(b2c, "xr0", [128, 2048], F32)
    put(b2c, "xr1", [128, 2048], F32)
    put(b2c, "x1sb0", [128, 2048], F32)
    put(b2c, "x1sb1", [128, 2048], F32)
    put(b2c, "wobB", [128, 2048], BF16)
    put(b2c, "rbsb", [128, 2048], BF16)
    P2_END = b2c.end

    b3c = Cur(SMALL_END)
    put(b3c, "x1h", [128, 8 * 2048], F32)
    put(b3c, "xh2", [128, 16 * 1024], BF16)
    put(b3c, "hfm", [128, 11 * 1024], BF16)
    put(b3c, "w2sb", [128, 11 * 2048], BF16)
    put(b3c, "w1d0", [128, 2048], BF16)
    put(b3c, "w1d1", [128, 2048], BF16)
    put(b3c, "tmpf", [128, 2048], F32)
    P3_END = b3c.end

    total = max(P1_END, P2_END, P3_END)
    assert total <= 208 * 1024, f"sbuf over budget: {total}"
    return L, total


def kts_for(qb):
    out = []
    for kt in range(4 * qb + 4):
        off = kt * 128 - qb * 512 if kt >= 4 * qb else 0
        out.append((kt, off))
    return out


def build_gi_map():
    m = {}
    c = 0
    for g in range(NG):
        for qb in range(NB):
            for pr in range(4):
                for i in range(len(kts_for(qb))):
                    m[(g, qb, pr, i)] = c
                    c += 1
    return m


GI = build_gi_map()
GI_AV = {v: k for k, v in GI.items()}


class Builder:
    def __init__(self, registry):
        self.reg = registry if registry is not None else {}
        self.pass0 = registry is None
        self.cnt = {}
        self.trace = []

    def sig(self, instr, sem, name, eng):
        inc = 16 if sem in ("dx", "dw", "dy", "dz") else 1
        c = self.cnt.get(sem, 0) + inc
        self.cnt[sem] = c
        instr.then_inc(self.sems[sem], inc)
        if name is not None and self.pass0:
            self.reg[name] = (sem, c)
        self.trace.append((eng, "s", sem, c))
        return instr

    def mark(self, name, sem):
        if self.pass0:
            self.reg[name] = (sem, self.cnt.get(sem, 0))

    def w(self, eng, engname, name):
        if name is None:
            return
        if name in self.reg:
            sem, c = self.reg[name]
            eng.wait_ge(self.sems[sem], c)
            self.trace.append((engname, "w", sem, c))
        else:
            assert self.pass0, f"unknown event {name}"

    def build(self):
        nc = bass.Bass("TRN2", target_bir_lowering=False)
        self.nc = nc
        din = {}
        din["x"] = nc.dram_tensor("x", [T, D], F32, kind="ExternalInput").ap()
        for nm, shp in [("wq", [16, 128, 2048]), ("wk", [16, 128, 2048]),
                        ("wv", [16, 128, 2048]), ("wo", [16, 128, 2048]),
                        ("qkb", [128, 32]), ("wvb", [128, 2048]), ("wob", [128, 2048]),
                        ("w1", [NE, 11, 128, 2048]), ("w2", [NE, 11, 128, 2048]),
                        ("b1", [128, NE * 11]), ("b2", [NE, 2048]),
                        ("cent", [128, 256]), ("cbb", [128, 16]),
                        ("ropeC", [16, 128, 2048]), ("ropeS", [16, 128, 2048]),
                        ("tri", [128, 128]), ("iden", [128, 128]),
                        ("onesr", [1, 128]), ("onesg", [2, 1024]),
                        ("ones2", [128, 2])]:
            din[nm] = nc.dram_tensor(nm, shp, F32, kind="ExternalInput").ap()
        self.din = din
        self.out = nc.dram_tensor("out", [T, D], F32, kind="ExternalOutput").ap()
        knd = "ExternalOutput" if DBG else "Internal"
        self.x1s = nc.dram_tensor("x1s", [T, D], F32, kind=knd).ap()
        self.ctxs = nc.dram_tensor("ctxs", [D, T], BF16, kind=knd).ap()
        self.rcps = nc.dram_tensor("rcps", [32, T], BF16, kind="Internal").ap()
        self.dbg = {}
        if DBG:
            for nm, shp, dt in [("d_xhat", [128, 16 * 2048], BF16),
                                ("d_q0", [128, 4 * 2048], BF16),
                                ("d_k0", [128, 4 * 2048], BF16),
                                ("d_v0", [128, 16 * 520], BF16),
                                ("d_expb", [128, 32], F32),
                                ("d_xh2", [128, 16 * 1024], BF16),
                                ("d_gall", [128, 128], F32),
                                ("d_hfm", [128, 11 * 1024], BF16),
                                ("d_rcps", [32, 2048], BF16),
                                ("d_ctxn", [128, 2048], BF16)]:
                self.dbg[nm] = nc.dram_tensor(nm, shp, dt, kind="ExternalOutput").ap()

        self.L, total = plan_layout()
        with nc.allow_low_precision(reason="bf16 matmul operand production"), \
             nc.sbuf_tensor("back", [128, total], mybir.dt.uint8) as back:
            self.tn = {}
            base = nc.lookup_mloc(back).addr
            for nm, (shp, dt, off) in self.L.items():
                self.tn[nm] = nc.alloc_sbuf_tensor_at(nm, shp, dt, offset=base + off)
            with (
                nc.psum_tensor("pbig0", [128, 2048], F32) as pbig0,
                nc.psum_tensor("pbig1", [128, 2048], F32) as pbig1,
                nc.semaphore("pe") as s_pe,
                nc.semaphore("sc") as s_sc,
                nc.semaphore("ve") as s_ve,
                nc.semaphore("dx") as s_dx,
                nc.semaphore("dw") as s_dw,
                nc.semaphore("dy") as s_dy,
                nc.semaphore("dz") as s_dz,
                nc.Block() as block,
            ):
                self.pbig = [pbig0, pbig1]
                self.pb = [pbig0[:, 0:512], pbig0[:, 512:1024],
                           pbig0[:, 1024:1536], pbig0[:, 1536:2048],
                           pbig1[:, 0:512], pbig1[:, 512:1024],
                           pbig1[:, 1024:1536], pbig1[:, 1536:2048]]
                self.sems = {"pe": s_pe, "sc": s_sc, "ve": s_ve, "dx": s_dx, "dw": s_dw, "dy": s_dy, "dz": s_dz}

                @block.tensor
                def _(t):
                    self.t_stream(t)

                @block.scalar
                def _(sc):
                    self.s_stream(sc)

                @block.vector
                def _(v):
                    self.v_stream(v)

                @block.sync
                def _(sy):
                    self.sy_stream(sy)

                @block.gpsimd
                def _(g):
                    self.g_stream(g)
        return nc

    def PT(self, gi):
        return self.tn[("ptP0", "ptP1", "ptP2")[gi % 3]]

    def t_stream(self, t):
        E = "pe"
        pb, X = self.pb, self.tn
        bankev = {}

        def bw(bid):
            self.w(t, E, bankev.get(bid))

        def xh(kc, a, b):
            return X["xhat"][:, kc * 2048 + a: kc * 2048 + b]

        self.w(t, E, "c:sync")
        for m in range(16):
            self.w(t, E, f"nrm0:{m}")
            xtf = X["xtf0"] if m % 2 == 0 else X["xtf1"]
            for b4 in range(4):
                b = b4 % 2
                bw(b)
                last = None
                for j in range(4):
                    kc = 4 * b4 + j
                    last = t.transpose(pb[b][:, j * 128:(j + 1) * 128],
                                       xtf[:, kc * 128:(kc + 1) * 128], X["idenf"][:])
                self.sig(last, "pe", f"tr0:{m}:{b4}", E)
                bankev[b] = f"trev0:{m}:{b4}"

        for g in range(NG):
            for mi in range(4):
                for xi, xn in enumerate(("q", "k")):
                    sl = (mi * 2 + xi) % 2
                    wblk = X["wqk0"] if sl == 0 else X["wqk1"]
                    self.w(t, E, f"w{xn}:{g}:{mi}")
                    for nb in range(NB):
                        if g == 0:
                            self.w(t, E, f"trev0:{4 * nb + 3}:3")
                        b = nb % 2
                        bw(b)
                        last = None
                        for kc in range(KC):
                            last = t.matmul(pb[b], wblk[:, kc * 128:(kc + 1) * 128],
                                            xh(kc, nb * 512, (nb + 1) * 512),
                                            start=(kc == 0), stop=(kc == KC - 1))
                        self.sig(last, "pe", f"qk:{g}:{xn}:{mi}:{nb}", E)
                        bankev[b] = f"qkevS:{g}:{xn}:{mi}:{nb}"
            self.w(t, E, f"wv:{g}")
            for mt in range(16):
                if g == 0:
                    self.w(t, E, f"trev0:{mt}:3")
                b = 2 + mt % 2
                bw(b)
                last = None
                for kc in range(KC):
                    last = t.matmul(pb[b], xh(kc, mt * 128, (mt + 1) * 128),
                                    X["wvg"][:, kc * 512:(kc + 1) * 512],
                                    start=(kc == 0), stop=(kc == KC - 1))
                self.sig(last, "pe", f"vv:{g}:{mt}", E)
                bankev[b] = f"vev:{g}:{mt}"
            for xn in ("q", "k"):
                stg = X["stg0"] if xn == "q" else X["stg1"]
                for mi in range(4):
                    self.w(t, E, f"qsq:{g}:{xn}:{mi}")
                    for nb in range(NB):
                        bw(4)
                        mm = t.matmul(self.pbig[1][0:2, 0:512], X["ones2b"][:],
                                      stg[:, nb * 512:(nb + 1) * 512],
                                      start=True, stop=True)
                        self.sig(mm, "pe", f"nrm:{g}:{xn}:{mi}:{nb}", E)
                        bankev[4] = f"nmax:{g}:{xn}:{mi}:{nb}"
            self.w(t, E, f"crow1:{g}")
            bw(4)
            mm = t.matmul(self.pbig[1][:, 0:4], X["onesf"][:],
                          X["crow"][0:1, 8 * g: 8 * g + 4], start=True, stop=True)
            self.sig(mm, "pe", f"expbm:{g}", E)
            bankev[4] = f"expbv:{g}"
            for qb in range(NB):
                for pr in range(4):
                    kl = kts_for(qb)
                    n = len(kl)
                    o2_ = 4 + 2 * (pr % 2)

                    def emit_S(i):
                        kt, off = kl[i]
                        diag = kt >= 4 * qb
                        sa, sb = 2 * (i % 2), 2 * (i % 2) + 1
                        bw(sa)
                        bw(sb)
                        q0 = pr * 2048 + qb * 512
                        t.matmul(pb[sa][:, off:512],
                                 X["K"][0:64, pr * 2048 + kt * 128: pr * 2048 + (kt + 1) * 128],
                                 X["Q"][0:64, q0 + off: q0 + 512],
                                 start=True, stop=not diag, tile_position=(0, 0),
                                 skip_group_check=True)
                        mm = t.matmul(pb[sb][:, off:512],
                                      X["K"][64:128, pr * 2048 + kt * 128: pr * 2048 + (kt + 1) * 128],
                                      X["Q"][64:128, q0 + off: q0 + 512],
                                      start=True, stop=not diag, tile_position=(64, 0),
                                      skip_group_check=True)
                        if diag:
                            t.matmul(pb[sa][:, off:off + 128], X["idenb"][:], X["trib"][:],
                                     start=False, stop=True, skip_group_check=True)
                            mm = t.matmul(pb[sb][:, off:off + 128], X["idenb"][:], X["trib"][:],
                                          start=False, stop=True, skip_group_check=True)
                        self.sig(mm, "pe", f"S:{g}:{qb}:{pr}:{i}", E)
                        bankev[sa] = f"exp:{g}:{qb}:{pr}:{i}"
                        bankev[sb] = f"exp:{g}:{qb}:{pr}:{i}"

                    bw(o2_)
                    bw(o2_ + 1)
                    emit_S(0)
                    if n > 1:
                        emit_S(1)
                    for i in range(n):
                        kt, off = kl[i]
                        gi = GI[(g, qb, pr, i)]
                        self.w(t, E, f"exp:{g}:{qb}:{pr}:{i}")
                        t.matmul(pb[o2_][0:65, off:512],
                                 X["V"][:, kt * 520 + (2 * pr) * 65: kt * 520 + (2 * pr) * 65 + 65],
                                 self.PT(gi)[:, off:512],
                                 start=(i == 0), stop=(i == n - 1),
                                 skip_group_check=True)
                        mm = t.matmul(pb[o2_ + 1][0:65, off:512],
                                      X["V"][:, kt * 520 + (2 * pr + 1) * 65: kt * 520 + (2 * pr + 1) * 65 + 65],
                                      self.PT(gi)[:, 512 + off:1024],
                                      start=(i == 0), stop=(i == n - 1),
                                      skip_group_check=True)
                        self.sig(mm, "pe", f"AVk:{g}:{qb}:{pr}:{i}", E)
                        if i + 2 < n:
                            emit_S(i + 2)
                    bankev[o2_] = f"ctx:{g}:{qb}:{pr}"
                    bankev[o2_ + 1] = f"ctx:{g}:{qb}:{pr}"

        self.w(t, E, "rbn:15")
        self.w(t, E, f"wo:{KC - 1}")
        for m in range(16):
            for nb in range(NB):
                b = nb
                bw(b)
                last = None
                for kc in range(KC):
                    last = t.matmul(pb[b],
                                    X["ctx"][:, kc * 2048 + m * 128: kc * 2048 + (m + 1) * 128],
                                    X["wosb"][:, kc * 2048 + nb * 512: kc * 2048 + (nb + 1) * 512],
                                    start=(kc == 0), stop=(kc == KC - 1))
                nm = "p2mmdone" if (m == 15 and nb == 3) else f"p2m:{m}:{nb}"
                self.sig(last, "pe", nm, E)
                bankev[b] = f"p2e1:{m}:{nb}"

        for hf in range(HF):
            for mt in range(MT):
                self.w(t, E, f"ln2a:{hf}:{mt}")
                for b4 in range(4):
                    b = 5 + b4 % 2
                    bw(b)
                    last = None
                    for j in range(4):
                        kc = 4 * b4 + j
                        last = t.transpose(pb[b][:, j * 128:(j + 1) * 128],
                                           X["tmpf"][:, kc * 128:(kc + 1) * 128],
                                           X["idenf"][:])
                    self.sig(last, "pe", f"tr2:{hf}:{mt}:{b4}", E)
                    bankev[b] = f"trev2:{hf}:{mt}:{b4}"
            for mt in range(MT):
                if hf == 0 and mt == 0:
                    self.w(t, E, "c:p3")
                for b4 in range(4):
                    self.w(t, E, f"trev2:{hf}:{mt}:{b4}")
                bw(4)
                last = None
                for kc in range(KC):
                    last = t.matmul(self.pbig[1][:, 0:16],
                                    X["xh2"][:, kc * 1024 + mt * 128: kc * 1024 + (mt + 1) * 128],
                                    X["centT"][:, kc * 16:(kc + 1) * 16],
                                    start=(kc == 0), stop=(kc == KC - 1))
                self.sig(last, "pe", f"sc:{hf}:{mt}", E)
                bankev[4] = f"sig:{hf}:{mt}"
                self.w(t, E, f"gbf:{hf}:{mt}")
                bw(7)
                mm = t.transpose(pb[7][0:16, 0:128], X["g_all"][:, mt * 16:(mt + 1) * 16],
                                 X["idenf"][:])
                self.sig(mm, "pe", f"gT:{hf}:{mt}", E)
                bankev[7] = f"gTe:{hf}:{mt}"
            self.w(t, E, f"trev2:{hf}:{MT - 1}:3")

            def emit_w1(e):
                for fc in range(FC):
                    wd = X["w1d0"] if fc % 2 == 0 else X["w1d1"]
                    self.w(t, E, f"w1:{hf}:{e}:{fc}")
                    for nb in range(2):
                        u = fc * 2 + nb
                        b = u % 2
                        bw(b)
                        last = None
                        for kc in range(KC):
                            last = t.matmul(pb[b], wd[:, kc * 128:(kc + 1) * 128],
                                            X["xh2"][:, kc * 1024 + nb * 512: kc * 1024 + (nb + 1) * 512],
                                            start=(kc == 0), stop=(kc == KC - 1))
                        self.sig(last, "pe", f"h:{hf}:{e}:{u}", E)
                        bankev[b] = f"relu:{hf}:{e}:{u}"

            def emit_w2(e, hb):
                hfmb = X["hfm"]
                self.w(t, E, f"w2:{hf}:{e}")
                self.w(t, E, f"relu:{hf}:{e}:{FC * 2 - 1}")
                for mt in range(MT):
                    for nb in range(NB):
                        u = mt * 4 + nb
                        b = 2 + u % 3
                        bw(b)
                        last = None
                        for fc in range(FC):
                            last = t.matmul(pb[b],
                                            hfmb[:, fc * 1024 + mt * 128: fc * 1024 + (mt + 1) * 128],
                                            X["w2sb"][:, fc * 2048 + nb * 512: fc * 2048 + (nb + 1) * 512],
                                            start=(fc == 0), stop=(fc == FC - 1))
                        self.sig(last, "pe", f"o:{hf}:{e}:{u}", E)
                        bankev[b] = f"oev:{hf}:{e}:{u}"

            for ei, e in enumerate(EORD):
                emit_w1(e)
                emit_w2(e, 0)
            for mt in range(MT):
                self.w(t, E, f"gTe:{hf}:{mt}")
                for nb in range(NB):
                    b = 5 + (mt * 4 + nb) % 2
                    bw(b)
                    mm = t.matmul(pb[b], X["gT"][:, mt * 128:(mt + 1) * 128],
                                  X["b2sb"][:, nb * 512:(nb + 1) * 512],
                                  start=True, stop=True)
                    self.sig(mm, "pe", f"B2:{hf}:{mt}:{nb}", E)
                    bankev[b] = f"b2ev:{hf}:{mt}:{nb}"

    def s_stream(self, sc):
        E = "sc"
        X, pb = self.tn, self.pb
        R = X["rows"]

        for m in range(16):
            self.w(sc, E, f"x0:{m}")
            stg = X["stg0"] if m % 2 == 0 else X["stg1"]
            xin = X["xin0"] if m % 2 == 0 else X["xin1"]
            i = sc.activation(stg[:], xin[:], AF.Square, accum_out=R[:, 1:2])
            self.sig(i, "sc", f"sq0:{m}", E)
            self.w(sc, E, f"var0:{m}")
            sc.activation(R[:, 4:5], R[:, 3:4], AF.Ln, bias=X["epsc"][:])
            i = sc.activation(R[:, 5:6], R[:, 4:5], AF.Exp, scale=-0.5)
            self.sig(i, "sc", f"rstd0:{m}", E)

        for g in range(NG):
            for xn in ("q", "k"):
                src = X["Q"] if xn == "q" else X["K"]
                stg = X["stg0"] if xn == "q" else X["stg1"]
                for mi in range(4):
                    self.w(sc, E, f"rope:{g}:{xn}:3" if mi == 0 else None)
                    self.w(sc, E, f"nrm:{g}:{xn}:{mi - 1}:3" if mi > 0 else None)
                    i = sc.activation(stg[:], src[:, mi * 2048:(mi + 1) * 2048], AF.Square)
                    self.sig(i, "sc", f"qsq:{g}:{xn}:{mi}", E)
            self.w(sc, E, f"crow0:{g}")
            sc.activation(X["crow"][0:1, 8 * g: 8 * g + 8],
                          X["crow"][0:1, 8 * g: 8 * g + 8], AF.Ln)
            i = sc.activation(X["crow"][0:1, 8 * g: 8 * g + 8],
                              X["crow"][0:1, 8 * g: 8 * g + 8], AF.Exp, scale=0.5)
            self.sig(i, "sc", f"crowS:{g}", E)
            self.w(sc, E, f"expbv:{g}")
            for qb in range(NB):
                for pr in range(4):
                    kl = kts_for(qb)
                    for i_, (kt, off) in enumerate(kl):
                        gi = GI[(g, qb, pr, i_)]
                        self.w(sc, E, f"S:{g}:{qb}:{pr}:{i_}")
                        if gi >= 3:
                            pg, pq, pp, pi = GI_AV[gi - 3]
                            self.w(sc, E, f"AVk:{pg}:{pq}:{pp}:{pi}")
                        j = i_ % 2
                        D = self.pbig[0][:, j * 1024:(j + 1) * 1024]
                        cP = X["expb"][:, 4 * g + pr: 4 * g + pr + 1]
                        if off == 0:
                            i2 = sc.activation(self.PT(gi)[:, 0:1024], D[:, 0:1024],
                                               AF.Exp, bias=cP, scale=0.125)
                        else:
                            sc.activation(self.PT(gi)[:, off:512], D[:, off:512],
                                          AF.Exp, bias=cP, scale=0.125)
                            i2 = sc.activation(self.PT(gi)[:, 512 + off:1024], D[:, 512 + off:1024],
                                               AF.Exp, bias=cP, scale=0.125)
                        self.sig(i2, "sc", f"exp:{g}:{qb}:{pr}:{i_}", E)

        for hf in range(HF):
            for mt in range(MT):
                self.w(sc, E, f"x1r:{hf}:{mt}")
                if mt > 0:
                    self.w(sc, E, f"tr2:{hf}:{mt - 1}:3")
                elif hf == 1:
                    self.w(sc, E, f"tr2:0:{MT - 1}:3")
                i = sc.activation(X["tmpf"][:], X["x1h"][:, mt * 2048:(mt + 1) * 2048],
                                  AF.Square, accum_out=R[:, 9:10])
                self.sig(i, "sc", f"sq2:{hf}:{mt}", E)
                self.w(sc, E, f"var2:{hf}:{mt}")
                sc.activation(R[:, 12:13], R[:, 11:12], AF.Ln, bias=X["epsc"][:])
                i = sc.activation(R[:, 13:14], R[:, 12:13], AF.Exp, scale=-0.5)
                self.sig(i, "sc", f"rstd2:{hf}:{mt}", E)
            for mt in range(MT):
                self.w(sc, E, f"sc:{hf}:{mt}")
                if hf + mt > 0:
                    pv = (hf, mt - 1) if mt > 0 else (hf - 1, MT - 1)
                    self.w(sc, E, f"gbf:{pv[0]}:{pv[1]}")
                i = sc.activation(X["scs"][:], self.pbig[1][:, 0:16], AF.Sigmoid)
                self.sig(i, "sc", f"sig:{hf}:{mt}", E)
            for ei, e in enumerate(EORD):
                gei = hf * NE + ei
                if gei >= 1:
                    hf2, ei2 = divmod(gei - 1, NE)
                    self.w(sc, E, f"o:{hf2}:{EORD[ei2]}:31")
                if hf == 0 and ei == 0:
                    self.w(sc, E, "c:p3")
                hfmb = X["hfm"]
                for fc in range(FC):
                    for nb in range(2):
                        u = fc * 2 + nb
                        self.w(sc, E, f"h:{hf}:{e}:{u}")
                        i = sc.activation(hfmb[:, fc * 1024 + nb * 512: fc * 1024 + (nb + 1) * 512],
                                          pb[u % 2], AF.Relu,
                                          bias=X["b1h"][:, e * 11 + fc: e * 11 + fc + 1])
                        self.sig(i, "sc", f"relu:{hf}:{e}:{u}", E)

    def v_stream(self, v):
        E = "ve"
        X, pb = self.tn, self.pb
        R = X["rows"]
        xhat_v = X["xhat"].ap().rearrange("p (c n) -> p c n", c=16)
        v.memset(X["epsc"][:], EPS)
        self.w(v, E, "c:sync")

        for m in range(16):
            self.w(v, E, f"x0:{m}")
            xin = X["xin0"] if m % 2 == 0 else X["xin1"]
            xtf = X["xtf0"] if m % 2 == 0 else X["xtf1"]
            v.reduce_sum(R[:, 0:1], xin[:], axis=AX.X)
            v.drain()
            v.tensor_scalar(R[:, 6:7], R[:, 0:1], 1.0 / D, None, ALU.mult)
            v.tensor_tensor(R[:, 2:3], R[:, 0:1], R[:, 0:1], ALU.mult)
            self.w(v, E, f"sq0:{m}")
            v.drain()
            v.tensor_scalar(R[:, 7:8], R[:, 2:3], 1.0 / (float(D) * D), None, ALU.mult)
            v.drain()
            i = v.tensor_scalar(R[:, 3:4], R[:, 1:2], 1.0 / D, R[:, 7:8],
                                ALU.mult, ALU.subtract)
            self.sig(i, "ve", f"var0:{m}", E)
            self.w(v, E, f"rstd0:{m}")
            self.w(v, E, f"tr0:{m - 2}:3" if m >= 2 else None)
            i = v.tensor_scalar(xtf[:], xin[:], R[:, 6:7], R[:, 5:6],
                                ALU.subtract, ALU.mult)
            self.sig(i, "ve", f"nrm0:{m}", E)
            for b4 in range(4):
                self.w(v, E, f"tr0:{m}:{b4}")
                src = pb[b4 % 2].rearrange("p (c n) -> p c n", n=128)
                i = v.tensor_copy(xhat_v[:, 4 * b4: 4 * b4 + 4, m * 128:(m + 1) * 128], src)
                self.sig(i, "ve", f"trev0:{m}:{b4}", E)

        ctxw_last = {}
        self.smw_hist = {}
        for g in range(NG):
            self.w(v, E, "c:gp" if g == 0 else None)
            for mi in range(4):
                c = 4 * g + mi
                ct = X["cosT0"] if c % 2 == 0 else X["cosT1"]
                st_ = X["sinT0"] if c % 2 == 0 else X["sinT1"]
                for xi, xn in enumerate(("q", "k")):
                    stg = X["stg0"] if xn == "q" else X["stg1"]
                    dst = X["Q"] if xn == "q" else X["K"]
                    if g > 0 and mi == 0:
                        self.w(v, E, f"nrm:{g - 1}:{xn}:3:3")
                    for nb in range(NB):
                        self.w(v, E, f"qk:{g}:{xn}:{mi}:{nb}")
                        i = v.tensor_scalar(stg[:, nb * 512:(nb + 1) * 512], pb[nb % 2],
                                            X["qkb"][:, xi * 16 + 4 * g + mi: xi * 16 + 4 * g + mi + 1],
                                            None, ALU.add)
                        self.sig(i, "ve", f"qkevS:{g}:{xn}:{mi}:{nb}", E)
                    v.drain()
                    if xi == 0:
                        self.w(v, E, f"rt:{c}")
                    d = dst[:, mi * 2048:(mi + 1) * 2048]
                    v.tensor_tensor(X["t1b"][:], stg[:], ct[:], ALU.mult)
                    for h2 in range(2):
                        b = 64 * h2
                        v.tensor_tensor(X["t2b"][b:b + 32, :], stg[b + 32:b + 64, :],
                                        st_[b + 32:b + 64, :], ALU.mult)
                        v.tensor_tensor(X["t2b"][b + 32:b + 64, :], stg[b:b + 32, :],
                                        st_[b:b + 32, :], ALU.mult)
                    v.drain()
                    i = v.tensor_tensor(d, X["t1b"][:], X["t2b"][:], ALU.add)
                    self.sig(i, "ve", f"rope:{g}:{xn}:{mi}", E)
            for mt in range(16):
                self.w(v, E, f"vv:{g}:{mt}")
                vt = X["V"][:, mt * 520:(mt + 1) * 520].rearrange("p (h c) -> p h c", c=65)
                i = v.tensor_tensor(vt[:, :, 0:64],
                                    pb[2 + mt % 2].rearrange("p (h c) -> p h c", c=64),
                                    X["bvb"][:, g * 512:(g + 1) * 512].rearrange("p (h c) -> p h c", c=64),
                                    ALU.add)
                v.memset(vt[:, :, 64:65], 1.0)
                self.sig(i, "ve", f"vev:{g}:{mt}", E)
            for xi, xn in enumerate(("q", "k")):
                for mi in range(4):
                    self.w(v, E, f"nmrow:{g - 1}" if (g > 0 and xi == 0 and mi == 0) else None)
                    for nb in range(NB):
                        self.w(v, E, f"nrm:{g}:{xn}:{mi}:{nb}")
                        if nb == 0:
                            i = v.reduce_max(R[0:2, 8:9], self.pbig[1][0:2, 0:512], axis=AX.X)
                        else:
                            v.reduce_max(R[0:2, 9:10], self.pbig[1][0:2, 0:512], axis=AX.X)
                            v.drain()
                            i = v.tensor_tensor(R[0:2, 8:9], R[0:2, 8:9], R[0:2, 9:10], ALU.max)
                        self.sig(i, "ve", f"nmax:{g}:{xn}:{mi}:{nb}", E)
                    v.drain()
                    i = v.tensor_copy(R[0:2, 10 + 4 * xi + mi: 11 + 4 * xi + mi], R[0:2, 8:9])
                    self.sig(i, "ve", f"nmaxF:{g}:{xn}:{mi}", E)
            self.w(v, E, f"nmrow:{g}")
            i = v.tensor_tensor(X["crow"][0:1, 8 * g:8 * g + 8], X["qrow"][0:1, 8 * g:8 * g + 8],
                                X["krow"][0:1, 8 * g:8 * g + 8], ALU.mult)
            self.sig(i, "ve", f"crow0:{g}", E)
            self.w(v, E, f"crowS:{g}")
            v.tensor_scalar(X["crow"][0:1, 8 * g:8 * g + 8], X["crow"][0:1, 8 * g:8 * g + 8],
                            -0.125 * 1.002, None, ALU.mult)
            v.drain()
            i = v.tensor_tensor(X["crow"][0:1, 8 * g:8 * g + 4], X["crow"][0:1, 8 * g:8 * g + 4],
                                X["crow"][0:1, 8 * g + 4:8 * g + 8], ALU.min)
            self.sig(i, "ve", f"crow1:{g}", E)
            self.w(v, E, f"expbm:{g}")
            i = v.tensor_copy(X["expb"][:, 4 * g: 4 * g + 4], self.pbig[1][:, 0:4])
            self.sig(i, "ve", f"expbv:{g}", E)
            for qb in range(NB):
                for pr in range(4):
                    kl = kts_for(qb)
                    n = len(kl)
                    o2_ = 4 + 2 * (pr % 2)
                    self.w(v, E, f"AVk:{g}:{qb}:{pr}:{n - 1}")
                    slot = pr % 2
                    self.w(v, E, ctxw_last.get(slot))
                    self.w(v, E, self.smw_hist.get(slot))
                    self.smw_hist[slot] = f"smw:{g}:{qb}:{pr}"
                    cstg = X["ctstg0"] if slot == 0 else X["ctstg1"]
                    smA = X["smA"] if slot == 0 else X["smA1"]
                    smB = X["smB"] if slot == 0 else X["smB1"]
                    v.tensor_copy(cstg[0:64, :], pb[o2_][0:64, :])
                    v.tensor_copy(cstg[64:128, :], pb[o2_ + 1][0:64, :])
                    v.reciprocal(smA[:], pb[o2_][64:65, :])
                    i = v.reciprocal(smB[:], pb[o2_ + 1][64:65, :])
                    self.sig(i, "ve", f"ctx:{g}:{qb}:{pr}", E)
                    ctxw_last[slot] = f"ctxw:{g}:{qb}:{pr}"

        for kc in range(KC):
            self.w(v, E, f"ctxr:{KC - 1}" if kc == 0 else None)
            self.w(v, E, f"rbd:{kc}")
            i = v.tensor_tensor(X["ctx"][:, kc * 2048:(kc + 1) * 2048],
                                X["ctx"][:, kc * 2048:(kc + 1) * 2048],
                                X["rbsb"][:], ALU.mult)
            self.sig(i, "ve", f"rbn:{kc}", E)
        for m in range(16):
            self.w(v, E, f"x2:{m}")
            self.w(v, E, f"x1w:{m - 2}" if m >= 2 else None)
            self.w(v, E, "wobB" if m == 0 else None)
            x1 = X["x1sb0"] if m % 2 == 0 else X["x1sb1"]
            xr = X["xr0"] if m % 2 == 0 else X["xr1"]
            for nb in range(NB):
                self.w(v, E, "p2mmdone" if (m == 15 and nb == 3) else f"p2m:{m}:{nb}")
                i = v.tensor_tensor(x1[:, nb * 512:(nb + 1) * 512], pb[nb],
                                    X["wobB"][:, nb * 512:(nb + 1) * 512], ALU.add)
                self.sig(i, "ve", f"p2e1:{m}:{nb}", E)
            last = None
            for nb in range(NB):
                last = v.tensor_tensor(x1[:, nb * 512:(nb + 1) * 512],
                                       x1[:, nb * 512:(nb + 1) * 512],
                                       xr[:, nb * 512:(nb + 1) * 512], ALU.add)
            self.sig(last, "ve", f"p2e2:{m}", E)

        for hf in range(HF):
            xh2_v = X["xh2"].ap().rearrange("p (c n) -> p c n", c=16)
            for mt in range(MT):
                self.w(v, E, f"x1r:{hf}:{mt}")
                x1 = X["x1h"][:, mt * 2048:(mt + 1) * 2048]
                v.reduce_sum(R[:, 8:9], x1, axis=AX.X)
                v.drain()
                v.tensor_scalar(R[:, 14:15], R[:, 8:9], 1.0 / D, None, ALU.mult)
                v.tensor_tensor(R[:, 10:11], R[:, 8:9], R[:, 8:9], ALU.mult)
                self.w(v, E, f"sq2:{hf}:{mt}")
                v.drain()
                v.tensor_scalar(R[:, 15:16], R[:, 10:11], 1.0 / (float(D) * D), None, ALU.mult)
                v.drain()
                i = v.tensor_scalar(R[:, 11:12], R[:, 9:10], 1.0 / D, R[:, 15:16],
                                    ALU.mult, ALU.subtract)
                self.sig(i, "ve", f"var2:{hf}:{mt}", E)
                self.w(v, E, f"rstd2:{hf}:{mt}")
                i = v.tensor_scalar(X["tmpf"][:], x1, R[:, 14:15], R[:, 13:14],
                                    ALU.subtract, ALU.mult)
                self.sig(i, "ve", f"ln2a:{hf}:{mt}", E)
                v.drain()
                i = v.tensor_tensor(x1, x1, X["tmpf"][:], ALU.add)
                self.sig(i, "ve", f"ln2b:{hf}:{mt}", E)
                for b4 in range(4):
                    self.w(v, E, f"tr2:{hf}:{mt}:{b4}")
                    src = pb[5 + b4 % 2].rearrange("p (c n) -> p c n", n=128)
                    i = v.tensor_copy(xh2_v[:, 4 * b4:4 * b4 + 4, mt * 128:(mt + 1) * 128], src)
                    self.sig(i, "ve", f"trev2:{hf}:{mt}:{b4}", E)
            for mt in range(MT):
                if mt == 0:
                    self.w(v, E, "c:p3" if hf == 0 else None)
                self.w(v, E, f"sig:{hf}:{mt}")
                v.tensor_tensor(X["scs"][:], X["scs"][:], X["cbb"][:], ALU.add)
                v.drain()
                v.tensor_copy(X["gt16"][:], X["scs"][:])
                v.drain()
                for r in range(3):
                    v.reduce_max(R[:, 16:17], X["scs"][:], axis=AX.X)
                    v.drain()
                    v.tensor_scalar(X["msk"][:], X["scs"][:], R[:, 16:17], None, ALU.is_ge)
                    v.drain()
                    v.scalar_tensor_tensor(X["scs"][:], X["msk"][:], NEG, X["scs"][:],
                                           ALU.mult, ALU.add)
                    v.drain()
                v.reduce_max(R[:, 20:21], X["scs"][:], axis=AX.X)
                v.drain()
                v.tensor_scalar(X["msk"][:], X["gt16"][:], R[:, 20:21], None, ALU.is_ge)
                v.drain()
                v.tensor_tensor(X["gt16"][:], X["gt16"][:], X["msk"][:], ALU.mult)
                v.drain()
                v.reduce_sum(R[:, 21:22], X["gt16"][:], axis=AX.X)
                v.drain()
                v.tensor_scalar(R[:, 21:22], R[:, 21:22], 1e-8, None, ALU.max)
                v.drain()
                v.reciprocal(R[:, 22:23], R[:, 21:22])
                v.drain()
                i = v.tensor_scalar(X["g_all"][:, mt * 16:(mt + 1) * 16], X["gt16"][:],
                                    R[:, 22:23], None, ALU.mult)
                self.sig(i, "ve", f"gbf:{hf}:{mt}", E)
                self.w(v, E, f"gT:{hf}:{mt}")
                i = v.tensor_copy(X["gT"][0:16, mt * 128:(mt + 1) * 128], pb[7][0:16, 0:128])
                self.sig(i, "ve", f"gTe:{hf}:{mt}", E)
            for ei, e in enumerate(EORD):
                for mt in range(MT):
                    for nb in range(NB):
                        u = mt * 4 + nb
                        self.w(v, E, f"o:{hf}:{e}:{u}")
                        bank = pb[2 + u % 3]
                        dst = X["x1h"][:, mt * 2048 + nb * 512: mt * 2048 + (nb + 1) * 512]
                        if e < NSH:
                            i = v.tensor_tensor(dst, bank[:], dst, ALU.add)
                        else:
                            gcol = X["g_all"][:, mt * 16 + (e - NSH): mt * 16 + (e - NSH) + 1]
                            i = v.scalar_tensor_tensor(dst, bank[:], gcol, dst,
                                                       ALU.mult, ALU.add)
                        self.sig(i, "ve", f"oev:{hf}:{e}:{u}", E)
            for mt in range(MT):
                for nb in range(NB):
                    self.w(v, E, f"B2:{hf}:{mt}:{nb}")
                    bank = pb[5 + (mt * 4 + nb) % 2]
                    dst = X["x1h"][:, mt * 2048 + nb * 512: mt * 2048 + (nb + 1) * 512]
                    i = v.tensor_tensor(dst, bank[:], dst, ALU.add)
                    self.sig(i, "ve", f"b2ev:{hf}:{mt}:{nb}", E)

    def sy_stream(self, sy):
        E = "sy"
        X, din = self.tn, self.din
        for nm, dst in (("iden", "idenf"), ("qkb", "qkb"), ("wvb", "bvb"), ("onesr", "onesf")):
            i = sy.dma_start(out=X[dst][:], in_=din[nm])
            self.sig(i, "dx", None, E)
        self.mark("c:sync", "dx")
        for m in range(16):
            if m >= 2:
                self.w(sy, E, f"nrm0:{m - 2}")
                self.w(sy, E, f"sq0:{m - 2}")
            xin = X["xin0"] if m % 2 == 0 else X["xin1"]
            i = sy.dma_start(out=xin[:], in_=din["x"][m * 128:(m + 1) * 128, :])
            self.sig(i, "dx", f"x0:{m}", E)
        if DBG:
            self.w(sy, E, "trev0:15:3")
            self.sig(sy.dma_start(out=self.dbg["d_xhat"], in_=X["xhat"][:]), "dx", None, E)
        for g in range(NG):
            if DBG and g == 1:
                self.w(sy, E, "rope:0:q:3")
                self.w(sy, E, "rope:0:k:3")
                self.w(sy, E, "vev:0:15")
                for nm, src in (("d_q0", "Q"), ("d_k0", "K"), ("d_v0", "V")):
                    self.sig(sy.dma_start(out=self.dbg[nm], in_=X[src][:]), "dx", None, E)
                self.w(sy, E, "expbv:0")
                self.sig(sy.dma_start(out=self.dbg["d_expb"], in_=X["expb"][:]), "dx", None, E)
            for xi, xn in enumerate(("q", "k")):
                for mi in range(4):
                    self.w(sy, E, f"nmaxF:{g}:{xn}:{mi}")
            for xi, xn in enumerate(("q", "k")):
                row = X["qrow"] if xn == "q" else X["krow"]
                for sel in range(2):
                    i = sy.dma_start(out=row[0:1, 8 * g + 4 * sel: 8 * g + 4 * sel + 4],
                                     in_=X["rows"][sel:sel + 1, 10 + 4 * xi: 14 + 4 * xi])
                    self.sig(i, "dx", None, E)
            self.mark(f"nmrow:{g}", "dx")
            for qb in range(NB):
                for pr in range(4):
                    self.w(sy, E, f"ctx:{g}:{qb}:{pr}")
                    cstg = X["ctstg0"] if pr % 2 == 0 else X["ctstg1"]
                    hA = 8 * g + 2 * pr
                    i = sy.dma_start(out=self.ctxs[hA * 64: hA * 64 + 128,
                                                   qb * 512:(qb + 1) * 512],
                                     in_=cstg[:])
                    self.sig(i, "dx", f"ctxw:{g}:{qb}:{pr}", E)
                    smA = X["smA"] if pr % 2 == 0 else X["smA1"]
                    smB = X["smB"] if pr % 2 == 0 else X["smB1"]
                    i = sy.dma_start(out=self.rcps[hA: hA + 1, qb * 512:(qb + 1) * 512],
                                     in_=smA[:])
                    self.sig(i, "dx", None, E)
                    i = sy.dma_start(out=self.rcps[hA + 1: hA + 2, qb * 512:(qb + 1) * 512],
                                     in_=smB[:])
                    self.sig(i, "dx", None, E)
                    self.mark(f"smw:{g}:{qb}:{pr}", "dx")
        sy.wait_ge(self.sems["dx"], self.cnt["dx"])
        self.trace.append((E, "w", "dx", self.cnt["dx"]))
        self.w(sy, E, "AVk:3:3:3:15")
        self.w(sy, E, "ctx:3:3:3")
        if DBG:
            self.sig(sy.dma_start(out=self.dbg["d_rcps"], in_=self.rcps), "dx", None, E)
        for kc in range(KC):
            i = sy.dma_start(out=X["ctx"][:, kc * 2048:(kc + 1) * 2048],
                             in_=self.ctxs[kc * 128:(kc + 1) * 128, :])
            self.sig(i, "dz", f"ctxr:{kc}" if kc == KC - 1 else None, E)
        for kc in range(KC):
            self.w(sy, E, f"rbn:{kc - 1}" if kc > 0 else None)
            src = bass.AP(self.rcps.tensor, 2 * kc * T,
                          [[T, 2], [0, 64], [1, T]])
            i = sy.dma_start(out=X["rbsb"][:], in_=src)
            self.sig(i, "dy", f"rbd:{kc}", E)
        if DBG:
            self.w(sy, E, "rbn:15")
            self.sig(sy.dma_start(out=self.dbg["d_ctxn"], in_=X["ctx"][:, 0:2048]), "dx", None, E)
        for m in range(16):
            self.w(sy, E, f"p2e2:{m - 2}" if m >= 2 else None)
            xr = X["xr0"] if m % 2 == 0 else X["xr1"]
            i = sy.dma_start(out=xr[:], in_=din["x"][m * 128:(m + 1) * 128, :])
            self.sig(i, "dx", f"x2:{m}", E)
            self.w(sy, E, f"p2e2:{m}")
            x1 = X["x1sb0"] if m % 2 == 0 else X["x1sb1"]
            i = sy.dma_start(out=self.x1s[m * 128:(m + 1) * 128, :], in_=x1[:])
            self.sig(i, "dx", f"x1w:{m}", E)
        for hf in range(HF):
            for mt in range(MT):
                mg = hf * MT + mt
                if hf == 0 and mt == 0:
                    self.w(sy, E, "p2mmdone")
                    self.w(sy, E, "p2e2:15")
                if hf == 1 and mt == 0:
                    self.w(sy, E, f"outw:0:{MT - 1}")
                    sy.wait_ge(self.sems["dx"], self.cnt["dx"])
                    self.trace.append((E, "w", "dx", self.cnt["dx"]))
                self.w(sy, E, f"x1w:{mg}")
                i = sy.dma_start(out=X["x1h"][:, mt * 2048:(mt + 1) * 2048],
                                 in_=self.x1s[mg * 128:(mg + 1) * 128, :])
                self.sig(i, "dx", f"x1r:{hf}:{mt}", E)
            if DBG and hf == 0:
                self.w(sy, E, f"trev2:0:{MT - 1}:3")
                self.sig(sy.dma_start(out=self.dbg["d_xh2"], in_=X["xh2"][:]), "dx", None, E)
                self.w(sy, E, f"gTe:0:{MT - 1}")
                self.sig(sy.dma_start(out=self.dbg["d_gall"], in_=X["g_all"][:]), "dx", None, E)
                self.w(sy, E, "relu:0:0:21")
                self.sig(sy.dma_start(out=self.dbg["d_hfm"], in_=X["hfm"][:]), "dx", None, E)
            for mt in range(MT):
                self.w(sy, E, f"b2ev:{hf}:{mt}:3")
                i = sy.dma_start(out=self.out[(hf * MT + mt) * 128:(hf * MT + mt + 1) * 128, :],
                                 in_=X["x1h"][:, mt * 2048:(mt + 1) * 2048])
                self.sig(i, "dx", f"outw:{hf}:{mt}", E)
        sy.wait_ge(self.sems["dx"], self.cnt["dx"])
        self.trace.append((E, "w", "dx", self.cnt["dx"]))

    def g_stream(self, g):
        E = "gp"
        X, din = self.tn, self.din
        for nm, dst in (("iden", "idenb"), ("tri", "trib"),
                        ("onesr", "onesb"), ("ones2", "ones2b")):
            i = g.dma_start(out=X[dst][:], in_=din[nm])
            self.sig(i, "dw", None, E)
        self.mark("c:gp", "dw")
        slotev = {}
        rtev = {}
        for gr in range(NG):
            for mi in range(4):
                c = 4 * gr + mi
                self.w(g, E, rtev.get(c % 2))
                i = g.dma_start(out=(X["cosT0"] if c % 2 == 0 else X["cosT1"])[:],
                                in_=din["ropeC"][c])
                self.sig(i, "dw", None, E)
                i = g.dma_start(out=(X["sinT0"] if c % 2 == 0 else X["sinT1"])[:],
                                in_=din["ropeS"][c])
                self.sig(i, "dw", f"rt:{c}", E)
                rtev[c % 2] = f"rope:{gr}:k:{mi}"
                for xi, xn in enumerate(("q", "k")):
                    sl = (mi * 2 + xi) % 2
                    self.w(g, E, slotev.get(sl))
                    i = g.dma_start(out=(X["wqk0"] if sl == 0 else X["wqk1"])[:],
                                    in_=din["w" + xn][4 * gr + mi])
                    self.sig(i, "dw", f"w{xn}:{gr}:{mi}", E)
                    slotev[sl] = f"qk:{gr}:{xn}:{mi}:3"
            if gr > 0:
                self.w(g, E, f"vv:{gr - 1}:15")
            else:
                self.w(g, E, "nrm0:15")
                self.w(g, E, "sq0:15")
            for kc in range(KC):
                i = g.dma_start(out=X["wvg"][:, kc * 512:(kc + 1) * 512],
                                in_=din["wv"][kc][:, gr * 512:(gr + 1) * 512])
                self.sig(i, "dw", None, E)
            self.mark(f"wv:{gr}", "dw")
        self.w(g, E, "vv:3:15")
        for kc in range(KC):
            i = g.dma_start(out=X["wosb"][:, kc * 2048:(kc + 1) * 2048], in_=din["wo"][kc])
            self.sig(i, "dw", f"wo:{kc}", E)
        self.w(g, E, "AVk:3:3:3:15")
        self.w(g, E, "ctx:3:3:3")
        i = g.dma_start(out=X["wobB"][:], in_=din["wob"])
        self.sig(i, "dw", "wobB", E)
        for nm, dst in (("b2", "b2sb"), ("cent", "centT"), ("b1", "b1h"), ("cbb", "cbb")):
            i = g.dma_start(out=X[dst][:], in_=din[nm])
            self.sig(i, "dw", None, E)
        i = g.dma_start(out=X["gT"][16:18, :], in_=din["onesg"])
        self.sig(i, "dw", None, E)
        self.mark("c:p3", "dw")
        st_local = {"first": True, "prev_e": None}
        w1slot = {}
        for hf in range(HF):
            def w1_block(e, fc):
                if st_local["first"]:
                    self.w(g, E, "p2mmdone")
                    self.w(g, E, "p2e2:15")
                    self.w(g, E, "x1w:15")
                    st_local["first"] = False
                self.w(g, E, w1slot.get(fc % 2))
                i = g.dma_start(out=(X["w1d0"] if fc % 2 == 0 else X["w1d1"])[:],
                                in_=din["w1"][e][fc])
                self.sig(i, "dw", f"w1:{hf}:{e}:{fc}", E)
                w1slot[fc % 2] = f"h:{hf}:{e}:{fc * 2 + 1}"

            def load_expert(e):
                w1_block(e, 0)
                w1_block(e, 1)
                if st_local["prev_e"] is not None:
                    pv = st_local["prev_e"]
                    self.w(g, E, f"o:{pv[0]}:{pv[1]}:31")
                w2j = 0
                for fc in range(2, FC):
                    for _ in range(2):
                        if w2j < FC:
                            i = g.dma_start(out=X["w2sb"][:, w2j * 2048:(w2j + 1) * 2048],
                                            in_=din["w2"][e][w2j])
                            self.sig(i, "dw", None, E)
                            w2j += 1
                    w1_block(e, fc)
                while w2j < FC:
                    i = g.dma_start(out=X["w2sb"][:, w2j * 2048:(w2j + 1) * 2048],
                                    in_=din["w2"][e][w2j])
                    self.sig(i, "dw", None, E)
                    w2j += 1
                self.mark(f"w2:{hf}:{e}", "dw")
                st_local["prev_e"] = (hf, e)

            for ei, e in enumerate(EORD):
                load_expert(e)


def check_deadlock(trace):
    streams = {}
    for ent in trace:
        streams.setdefault(ent[0], []).append(ent)
    ptrs = {k: 0 for k in streams}
    sems = {}
    progress = True
    while progress:
        progress = False
        for k, lst in streams.items():
            while ptrs[k] < len(lst):
                eng, kind, sem, val = lst[ptrs[k]]
                if kind == "s":
                    sems[sem] = max(sems.get(sem, 0), val)
                    ptrs[k] += 1
                    progress = True
                else:
                    if sems.get(sem, 0) >= val:
                        ptrs[k] += 1
                        progress = True
                    else:
                        break
    stuck = {k: (ptrs[k], len(streams[k])) for k in streams if ptrs[k] < len(streams[k])}
    if stuck:
        detail = {k: streams[k][v[0]] for k, v in stuck.items()}
        raise RuntimeError(f"deadlock: {stuck} heads={detail}")


_CACHE = {}


def _get_nc():
    if "nc" not in _CACHE:
        b0 = Builder(None)
        b0.build()
        b1 = Builder(b0.reg)
        nc = b1.build()
        check_deadlock(b1.trace)
        _CACHE["nc"] = nc
    return _CACHE["nc"]


def kernel(**inputs):
    hp = host_prepare(inputs)
    x = np.asarray(inputs["L0_attn_x"], np.float32)
    nc = _get_nc()
    in_maps = []
    for c in range(NCORE):
        m = dict(hp)
        m["x"] = _t(x[c])
        in_maps.append(m)
    res = run_bass_kernel_spmd(nc, in_maps, core_ids=list(range(NCORE)))
    return np.stack([np.asarray(res.results[c]["out"]) for c in range(NCORE)], axis=0)


# version 13
# speedup vs baseline: 1.0964x; 1.0964x over previous
import numpy as np
import concourse.bass as bass
import concourse.mybir as mybir
from concourse.bass_utils import run_bass_kernel_spmd

F32 = mybir.dt.float32
BF16 = mybir.dt.bfloat16
AF = mybir.ActivationFunctionType
ALU = mybir.AluOpType
AX = mybir.AxisListType

D = 2048
T = 2048
NCORE = 8
NG = 4
KC = 16
NB = 4
FC = 11
NSH = 2
NE = 18
HF = 2
TH = T // HF
MT = TH // 128
EPS = 1e-5
NEG = -1.0e30
EORD = list(range(NE))
DBG = False


def _t(a):
    return np.ascontiguousarray(a)


def host_prepare(i):
    d = {}
    for nm, w in (("wq", i["L0_attn_wq_w"]), ("wk", i["L0_attn_wk_w"])):
        d[nm] = _t(np.asarray(w, np.float32).reshape(16, 128, 16, 128)
                   .transpose(0, 3, 2, 1).reshape(16, 128, 2048))
    d["wv"] = _t(np.asarray(i["L0_attn_wv_w"], np.float32).T.reshape(16, 128, 2048))
    d["wo"] = _t(np.asarray(i["L0_attn_wo_w"], np.float32).T.reshape(16, 128, 2048))
    qb = np.asarray(i["L0_attn_wq_b"], np.float32).reshape(16, 128).T
    kb = np.asarray(i["L0_attn_wk_b"], np.float32).reshape(16, 128).T
    d["qkb"] = _t(np.concatenate([qb, kb], axis=1))
    d["wvb"] = _t(np.broadcast_to(np.asarray(i["L0_attn_wv_b"], np.float32), (128, 2048)))
    d["wob"] = _t(np.broadcast_to(np.asarray(i["L0_attn_wo_b"], np.float32), (128, 2048)))
    w1 = np.concatenate([np.asarray(i["L0_ffn_shared_w1"], np.float32),
                         np.asarray(i["L0_ffn_routed_w1"], np.float32)], axis=0)
    w2 = np.concatenate([np.asarray(i["L0_ffn_shared_w2"], np.float32),
                         np.asarray(i["L0_ffn_routed_w2"], np.float32)], axis=0)
    b1 = np.concatenate([np.asarray(i["L0_ffn_shared_b1"], np.float32),
                         np.asarray(i["L0_ffn_routed_b1"], np.float32)], axis=0)
    b2 = np.concatenate([np.asarray(i["L0_ffn_shared_b2"], np.float32),
                         np.asarray(i["L0_ffn_routed_b2"], np.float32)], axis=0)
    d["w1"] = _t(w1.reshape(NE, 16, 128, 11, 128).transpose(0, 3, 2, 1, 4)
                 .reshape(NE, 11, 128, 2048))
    d["w2"] = _t(w2.reshape(NE, 11, 128, 2048))
    d["b1"] = _t(b1.reshape(NE, 11, 128).transpose(2, 0, 1).reshape(128, NE * 11))
    d["b2"] = _t(b2)
    d["cent"] = _t(np.asarray(i["L0_ffn_centroids"], np.float32).T
                   .reshape(16, 128, 16).transpose(1, 0, 2).reshape(128, 256))
    d["cbb"] = _t(np.broadcast_to(np.asarray(i["L0_ffn_bias"], np.float32), (128, 16)))
    pos = np.arange(T, dtype=np.float64)
    fr = np.arange(32, dtype=np.float64) / 32.0
    ang = pos[None, :] / (10000.0 ** fr)[:, None]
    ch = np.cos(ang).astype(np.float32)
    sh = np.sin(ang).astype(np.float32)
    rc = np.zeros((16, 128, T), np.float32)
    rs = np.zeros((16, 128, T), np.float32)
    for c in range(16):
        rc[c, 0:64] = ch[2 * c]
        rc[c, 64:128] = ch[2 * c + 1]
        rs[c, 0:32] = sh[2 * c]
        rs[c, 32:64] = -sh[2 * c]
        rs[c, 64:96] = sh[2 * c + 1]
        rs[c, 96:128] = -sh[2 * c + 1]
    d["ropeC"] = _t(rc)
    d["ropeS"] = _t(rs)
    tri = np.where(np.arange(128)[:, None] <= np.arange(128)[None, :], 0.0, NEG)
    d["tri"] = _t(tri.astype(np.float32))
    d["iden"] = _t(np.eye(128, dtype=np.float32))
    d["onesr"] = _t(np.ones((1, 128), np.float32))
    d["onesg"] = _t(np.ones((2, 1024), np.float32))
    o2 = np.zeros((128, 2), np.float32)
    o2[0:64, 0] = 1.0
    o2[64:128, 1] = 1.0
    d["ones2"] = _t(o2)
    return d


AL = 32


class Cur:
    def __init__(self, start):
        self.o = (start + AL - 1) // AL * AL
        self.end = self.o

    def take(self, nbytes):
        o = self.o
        self.o = (o + nbytes + AL - 1) // AL * AL
        self.end = max(self.end, self.o)
        return o


def plan_layout():
    L = {}

    def put(cur, name, shape, dtype, at=None):
        nb = int(np.prod(shape[1:])) * mybir.dt.size(dtype)
        off = cur.take(nb) if at is None else at
        L[name] = (shape, dtype, off)

    perm = Cur(0)
    put(perm, "idenf", [128, 128], F32)
    put(perm, "idenb", [128, 128], BF16)
    put(perm, "onesb", [1, 128], BF16)
    put(perm, "onesf", [1, 128], F32)
    put(perm, "ones2b", [128, 2], BF16)
    put(perm, "rows", [128, 96], F32)
    put(perm, "epsc", [128, 1], F32)
    put(perm, "qrow", [1, 32], F32)
    put(perm, "krow", [1, 32], F32)
    put(perm, "crow", [1, 32], F32)
    put(perm, "rcpA", [1, 512], BF16)
    put(perm, "rcpB", [1, 512], BF16)
    PERM_END = perm.end

    s1 = Cur(PERM_END)
    put(s1, "cosT0", [128, 2048], BF16)
    put(s1, "cosT1", [128, 2048], BF16)
    put(s1, "sinT0", [128, 2048], BF16)
    put(s1, "sinT1", [128, 2048], BF16)
    put(s1, "trib", [128, 128], BF16)
    put(s1, "qkb", [128, 32], F32)
    put(s1, "expb", [128, 32], F32)
    s3 = Cur(PERM_END)
    put(s3, "b2sb", [18, 2048], BF16)
    put(s3, "centT", [128, 256], BF16)
    put(s3, "b1h", [128, NE * 11], F32)
    put(s3, "cbb", [128, 16], F32)
    put(s3, "gT", [18, 1024], BF16)
    put(s3, "g_all", [128, 128], F32)
    put(s3, "scs", [128, 16], F32)
    put(s3, "msk", [128, 16], F32)
    put(s3, "gt16", [128, 16], F32)
    SMALL_END = max(s1.end, s3.end)

    b1c = Cur(SMALL_END)
    put(b1c, "xhat", [128, 16 * 2048], BF16)
    P2_START = b1c.o
    put(b1c, "Q", [128, 4 * 2048], BF16)
    put(b1c, "K", [128, 4 * 2048], BF16)
    put(b1c, "V", [128, 16 * 520], BF16)
    put(b1c, "stg0", [128, 2048], BF16)
    put(b1c, "stg1", [128, 2048], BF16)
    put(b1c, "t1b", [128, 2048], BF16)
    put(b1c, "t2b", [128, 2048], BF16)
    put(b1c, "wvg", [128, 16 * 512], BF16)
    put(b1c, "wqk0", [128, 2048], BF16)
    put(b1c, "wqk1", [128, 2048], BF16)
    put(b1c, "bvb", [128, 2048], F32)
    put(b1c, "smA", [1, 512], BF16)
    put(b1c, "smB", [1, 512], BF16)
    put(b1c, "smA1", [1, 512], BF16)
    put(b1c, "smB1", [1, 512], BF16)
    put(b1c, "ctstg0", [128, 512], BF16)
    put(b1c, "ctstg1", [128, 512], BF16)
    put(b1c, "ptP0", [128, 1024], BF16)
    put(b1c, "ptP1", [128, 1024], BF16)
    put(b1c, "ptP2", [128, 1024], BF16)
    L["xtf0"] = ([128, 2048], F32, L["Q"][2])
    L["xtf1"] = ([128, 2048], F32, L["Q"][2] + 8192)
    L["xin0"] = ([128, 2048], F32, L["t1b"][2])
    L["xin1"] = ([128, 2048], F32, L["wvg"][2])
    P1_END = b1c.end

    b2c = Cur(P2_START)
    put(b2c, "wosb", [128, 16 * 2048], BF16, at=L["xhat"][2])
    put(b2c, "ctx", [128, 16 * 2048], BF16)
    put(b2c, "xr0", [128, 2048], F32)
    put(b2c, "xr1", [128, 2048], F32)
    put(b2c, "x1sb0", [128, 2048], F32)
    put(b2c, "x1sb1", [128, 2048], F32)
    put(b2c, "wobB", [128, 2048], BF16)
    put(b2c, "rbsb", [128, 2048], BF16)
    P2_END = b2c.end

    b3c = Cur(SMALL_END)
    put(b3c, "x1h", [128, 8 * 2048], F32)
    put(b3c, "xh2", [128, 16 * 1024], BF16)
    put(b3c, "hfm", [128, 11 * 1024], BF16)
    put(b3c, "w2sb", [128, 11 * 2048], BF16)
    put(b3c, "w1d0", [128, 2048], BF16)
    put(b3c, "w1d1", [128, 2048], BF16)
    put(b3c, "tmpf", [128, 2048], F32)
    P3_END = b3c.end

    total = max(P1_END, P2_END, P3_END)
    assert total <= 208 * 1024, f"sbuf over budget: {total}"
    return L, total


def kts_for(qb):
    out = []
    for kt in range(4 * qb + 4):
        off = kt * 128 - qb * 512 if kt >= 4 * qb else 0
        out.append((kt, off))
    return out


def build_gi_map():
    m = {}
    c = 0
    for g in range(NG):
        for qb in range(NB):
            for pr in range(4):
                for i in range(len(kts_for(qb))):
                    m[(g, qb, pr, i)] = c
                    c += 1
    return m


GI = build_gi_map()
GI_AV = {v: k for k, v in GI.items()}


class Builder:
    def __init__(self, registry):
        self.reg = registry if registry is not None else {}
        self.pass0 = registry is None
        self.cnt = {}
        self.trace = []

    def sig(self, instr, sem, name, eng):
        inc = 16 if sem in ("dx", "dw", "dy", "dz") else 1
        c = self.cnt.get(sem, 0) + inc
        self.cnt[sem] = c
        instr.then_inc(self.sems[sem], inc)
        if name is not None and self.pass0:
            self.reg[name] = (sem, c)
        self.trace.append((eng, "s", sem, c))
        return instr

    def mark(self, name, sem):
        if self.pass0:
            self.reg[name] = (sem, self.cnt.get(sem, 0))

    def w(self, eng, engname, name):
        if name is None:
            return
        if name in self.reg:
            sem, c = self.reg[name]
            eng.wait_ge(self.sems[sem], c)
            self.trace.append((engname, "w", sem, c))
        else:
            assert self.pass0, f"unknown event {name}"

    def build(self):
        nc = bass.Bass("TRN2", target_bir_lowering=False)
        self.nc = nc
        din = {}
        din["x"] = nc.dram_tensor("x", [T, D], F32, kind="ExternalInput").ap()
        for nm, shp in [("wq", [16, 128, 2048]), ("wk", [16, 128, 2048]),
                        ("wv", [16, 128, 2048]), ("wo", [16, 128, 2048]),
                        ("qkb", [128, 32]), ("wvb", [128, 2048]), ("wob", [128, 2048]),
                        ("w1", [NE, 11, 128, 2048]), ("w2", [NE, 11, 128, 2048]),
                        ("b1", [128, NE * 11]), ("b2", [NE, 2048]),
                        ("cent", [128, 256]), ("cbb", [128, 16]),
                        ("ropeC", [16, 128, 2048]), ("ropeS", [16, 128, 2048]),
                        ("tri", [128, 128]), ("iden", [128, 128]),
                        ("onesr", [1, 128]), ("onesg", [2, 1024]),
                        ("ones2", [128, 2])]:
            din[nm] = nc.dram_tensor(nm, shp, F32, kind="ExternalInput").ap()
        self.din = din
        self.out = nc.dram_tensor("out", [T, D], F32, kind="ExternalOutput").ap()
        knd = "ExternalOutput" if DBG else "Internal"
        self.x1s = nc.dram_tensor("x1s", [T, D], F32, kind=knd).ap()
        self.ctxs = nc.dram_tensor("ctxs", [D, T], BF16, kind=knd).ap()
        self.rcps = nc.dram_tensor("rcps", [32, T], BF16, kind="Internal").ap()
        self.woc = nc.dram_tensor("woc", [16, 128, 2048], BF16, kind="Internal").ap()
        self.dbg = {}
        if DBG:
            for nm, shp, dt in [("d_xhat", [128, 16 * 2048], BF16),
                                ("d_q0", [128, 4 * 2048], BF16),
                                ("d_k0", [128, 4 * 2048], BF16),
                                ("d_v0", [128, 16 * 520], BF16),
                                ("d_expb", [128, 32], F32),
                                ("d_xh2", [128, 16 * 1024], BF16),
                                ("d_gall", [128, 128], F32),
                                ("d_hfm", [128, 11 * 1024], BF16),
                                ("d_rcps", [32, 2048], BF16),
                                ("d_ctxn", [128, 2048], BF16)]:
                self.dbg[nm] = nc.dram_tensor(nm, shp, dt, kind="ExternalOutput").ap()

        self.L, total = plan_layout()
        with nc.allow_low_precision(reason="bf16 matmul operand production"), \
             nc.sbuf_tensor("back", [128, total], mybir.dt.uint8) as back:
            self.tn = {}
            base = nc.lookup_mloc(back).addr
            for nm, (shp, dt, off) in self.L.items():
                self.tn[nm] = nc.alloc_sbuf_tensor_at(nm, shp, dt, offset=base + off)
            with (
                nc.psum_tensor("pbig0", [128, 2048], F32) as pbig0,
                nc.psum_tensor("pbig1", [128, 2048], F32) as pbig1,
                nc.semaphore("pe") as s_pe,
                nc.semaphore("sc") as s_sc,
                nc.semaphore("ve") as s_ve,
                nc.semaphore("dx") as s_dx,
                nc.semaphore("dw") as s_dw,
                nc.semaphore("dy") as s_dy,
                nc.semaphore("dz") as s_dz,
                nc.Block() as block,
            ):
                self.pbig = [pbig0, pbig1]
                self.pb = [pbig0[:, 0:512], pbig0[:, 512:1024],
                           pbig0[:, 1024:1536], pbig0[:, 1536:2048],
                           pbig1[:, 0:512], pbig1[:, 512:1024],
                           pbig1[:, 1024:1536], pbig1[:, 1536:2048]]
                self.sems = {"pe": s_pe, "sc": s_sc, "ve": s_ve, "dx": s_dx, "dw": s_dw, "dy": s_dy, "dz": s_dz}

                @block.tensor
                def _(t):
                    self.t_stream(t)

                @block.scalar
                def _(sc):
                    self.s_stream(sc)

                @block.vector
                def _(v):
                    self.v_stream(v)

                @block.sync
                def _(sy):
                    self.sy_stream(sy)

                @block.gpsimd
                def _(g):
                    self.g_stream(g)
        return nc

    def PT(self, gi):
        return self.tn[("ptP0", "ptP1", "ptP2")[gi % 3]]

    def t_stream(self, t):
        E = "pe"
        pb, X = self.pb, self.tn
        bankev = {}

        def bw(bid):
            self.w(t, E, bankev.get(bid))

        def xh(kc, a, b):
            return X["xhat"][:, kc * 2048 + a: kc * 2048 + b]

        self.w(t, E, "c:sync")
        for m in range(16):
            self.w(t, E, f"nrm0:{m}")
            xtf = X["xtf0"] if m % 2 == 0 else X["xtf1"]
            for b4 in range(4):
                b = b4 % 2
                bw(b)
                last = None
                for j in range(4):
                    kc = 4 * b4 + j
                    last = t.transpose(pb[b][:, j * 128:(j + 1) * 128],
                                       xtf[:, kc * 128:(kc + 1) * 128], X["idenf"][:])
                self.sig(last, "pe", f"tr0:{m}:{b4}", E)
                bankev[b] = f"trev0:{m}:{b4}"

        for g in range(NG):
            for mi in range(4):
                for xi, xn in enumerate(("q", "k")):
                    sl = (mi * 2 + xi) % 2
                    wblk = X["wqk0"] if sl == 0 else X["wqk1"]
                    self.w(t, E, f"w{xn}:{g}:{mi}")
                    for nb in range(NB):
                        if g == 0:
                            self.w(t, E, f"trev0:{4 * nb + 3}:3")
                        b = nb % 2
                        bw(b)
                        last = None
                        for kc in range(KC):
                            last = t.matmul(pb[b], wblk[:, kc * 128:(kc + 1) * 128],
                                            xh(kc, nb * 512, (nb + 1) * 512),
                                            start=(kc == 0), stop=(kc == KC - 1))
                        self.sig(last, "pe", f"qk:{g}:{xn}:{mi}:{nb}", E)
                        bankev[b] = f"qkevS:{g}:{xn}:{mi}:{nb}"
            self.w(t, E, f"wv:{g}")
            for mt in range(16):
                if g == 0:
                    self.w(t, E, f"trev0:{mt}:3")
                b = 2 + mt % 2
                bw(b)
                last = None
                for kc in range(KC):
                    last = t.matmul(pb[b], xh(kc, mt * 128, (mt + 1) * 128),
                                    X["wvg"][:, kc * 512:(kc + 1) * 512],
                                    start=(kc == 0), stop=(kc == KC - 1))
                self.sig(last, "pe", f"vv:{g}:{mt}", E)
                bankev[b] = f"vev:{g}:{mt}"
            for xn in ("q", "k"):
                stg = X["stg0"] if xn == "q" else X["stg1"]
                for mi in range(4):
                    self.w(t, E, f"qsq:{g}:{xn}:{mi}")
                    for nb in range(NB):
                        bw(4 + nb)
                        bankev[4 + nb] = f"nmax:{g}:{xn}:{mi}"
                    last = None
                    for nb in range(NB):
                        last = t.matmul(self.pbig[1][0:2, nb * 512:(nb + 1) * 512],
                                        X["ones2b"][:],
                                        stg[:, nb * 512:(nb + 1) * 512],
                                        start=True, stop=True)
                    self.sig(last, "pe", f"nrm:{g}:{xn}:{mi}", E)
            self.w(t, E, f"crow1:{g}")
            bw(4)
            mm = t.matmul(self.pbig[1][:, 0:4], X["onesf"][:],
                          X["crow"][0:1, 8 * g: 8 * g + 4], start=True, stop=True)
            self.sig(mm, "pe", f"expbm:{g}", E)
            bankev[4] = f"expbv:{g}"
            for qb in range(NB):
                for pr in range(4):
                    kl = kts_for(qb)
                    n = len(kl)
                    o2_ = 4 + 2 * (pr % 2)

                    def emit_S(i):
                        kt, off = kl[i]
                        diag = kt >= 4 * qb
                        sa, sb = 2 * (i % 2), 2 * (i % 2) + 1
                        bw(sa)
                        bw(sb)
                        q0 = pr * 2048 + qb * 512
                        t.matmul(pb[sa][:, off:512],
                                 X["K"][0:64, pr * 2048 + kt * 128: pr * 2048 + (kt + 1) * 128],
                                 X["Q"][0:64, q0 + off: q0 + 512],
                                 start=True, stop=not diag, tile_position=(0, 0),
                                 skip_group_check=True)
                        mm = t.matmul(pb[sb][:, off:512],
                                      X["K"][64:128, pr * 2048 + kt * 128: pr * 2048 + (kt + 1) * 128],
                                      X["Q"][64:128, q0 + off: q0 + 512],
                                      start=True, stop=not diag, tile_position=(64, 0),
                                      skip_group_check=True)
                        if diag:
                            t.matmul(pb[sa][:, off:off + 128], X["idenb"][:], X["trib"][:],
                                     start=False, stop=True, skip_group_check=True)
                            mm = t.matmul(pb[sb][:, off:off + 128], X["idenb"][:], X["trib"][:],
                                          start=False, stop=True, skip_group_check=True)
                        self.sig(mm, "pe", f"S:{g}:{qb}:{pr}:{i}", E)
                        bankev[sa] = f"exp:{g}:{qb}:{pr}:{i}"
                        bankev[sb] = f"exp:{g}:{qb}:{pr}:{i}"

                    bw(o2_)
                    bw(o2_ + 1)
                    emit_S(0)
                    if n > 1:
                        emit_S(1)
                    for i in range(n):
                        kt, off = kl[i]
                        gi = GI[(g, qb, pr, i)]
                        self.w(t, E, f"exp:{g}:{qb}:{pr}:{i}")
                        t.matmul(pb[o2_][0:65, off:512],
                                 X["V"][:, kt * 520 + (2 * pr) * 65: kt * 520 + (2 * pr) * 65 + 65],
                                 self.PT(gi)[:, off:512],
                                 start=(i == 0), stop=(i == n - 1),
                                 skip_group_check=True)
                        mm = t.matmul(pb[o2_ + 1][0:65, off:512],
                                      X["V"][:, kt * 520 + (2 * pr + 1) * 65: kt * 520 + (2 * pr + 1) * 65 + 65],
                                      self.PT(gi)[:, 512 + off:1024],
                                      start=(i == 0), stop=(i == n - 1),
                                      skip_group_check=True)
                        self.sig(mm, "pe", f"AVk:{g}:{qb}:{pr}:{i}", E)
                        if i + 2 < n:
                            emit_S(i + 2)
                    bankev[o2_] = f"ctx:{g}:{qb}:{pr}"
                    bankev[o2_ + 1] = f"ctx:{g}:{qb}:{pr}"

        self.w(t, E, "rbn:15")
        self.w(t, E, f"wo:{KC - 1}")
        for m in range(16):
            for nb in range(NB):
                b = nb
                bw(b)
                last = None
                for kc in range(KC):
                    last = t.matmul(pb[b],
                                    X["ctx"][:, kc * 2048 + m * 128: kc * 2048 + (m + 1) * 128],
                                    X["wosb"][:, kc * 2048 + nb * 512: kc * 2048 + (nb + 1) * 512],
                                    start=(kc == 0), stop=(kc == KC - 1))
                nm = "p2mmdone" if (m == 15 and nb == 3) else f"p2m:{m}:{nb}"
                self.sig(last, "pe", nm, E)
                bankev[b] = f"p2e1:{m}:{nb}"

        for hf in range(HF):
            for mt in range(MT):
                self.w(t, E, f"ln2a:{hf}:{mt}")
                for b4 in range(4):
                    b = 5 + b4 % 2
                    bw(b)
                    last = None
                    for j in range(4):
                        kc = 4 * b4 + j
                        last = t.transpose(pb[b][:, j * 128:(j + 1) * 128],
                                           X["tmpf"][:, kc * 128:(kc + 1) * 128],
                                           X["idenf"][:])
                    self.sig(last, "pe", f"tr2:{hf}:{mt}:{b4}", E)
                    bankev[b] = f"trev2:{hf}:{mt}:{b4}"
            for mt in range(MT):
                if hf == 0 and mt == 0:
                    self.w(t, E, "c:p3")
                for b4 in range(4):
                    self.w(t, E, f"trev2:{hf}:{mt}:{b4}")
                bw(4)
                last = None
                for kc in range(KC):
                    last = t.matmul(self.pbig[1][:, 0:16],
                                    X["xh2"][:, kc * 1024 + mt * 128: kc * 1024 + (mt + 1) * 128],
                                    X["centT"][:, kc * 16:(kc + 1) * 16],
                                    start=(kc == 0), stop=(kc == KC - 1))
                self.sig(last, "pe", f"sc:{hf}:{mt}", E)
                bankev[4] = f"sig:{hf}:{mt}"
                self.w(t, E, f"gbf:{hf}:{mt}")
                bw(7)
                mm = t.transpose(pb[7][0:16, 0:128], X["g_all"][:, mt * 16:(mt + 1) * 16],
                                 X["idenf"][:])
                self.sig(mm, "pe", f"gT:{hf}:{mt}", E)
                bankev[7] = f"gTe:{hf}:{mt}"
            self.w(t, E, f"trev2:{hf}:{MT - 1}:3")

            def emit_w1(e):
                for fc in range(FC):
                    wd = X["w1d0"] if fc % 2 == 0 else X["w1d1"]
                    self.w(t, E, f"w1:{hf}:{e}:{fc}")
                    for nb in range(2):
                        u = fc * 2 + nb
                        b = u % 2
                        bw(b)
                        last = None
                        for kc in range(KC):
                            last = t.matmul(pb[b], wd[:, kc * 128:(kc + 1) * 128],
                                            X["xh2"][:, kc * 1024 + nb * 512: kc * 1024 + (nb + 1) * 512],
                                            start=(kc == 0), stop=(kc == KC - 1))
                        self.sig(last, "pe", f"h:{hf}:{e}:{u}", E)
                        bankev[b] = f"relu:{hf}:{e}:{u}"

            def emit_w2(e, hb):
                hfmb = X["hfm"]
                self.w(t, E, f"w2:{hf}:{e}")
                self.w(t, E, f"relu:{hf}:{e}:{FC * 2 - 1}")
                for mt in range(MT):
                    for nb in range(NB):
                        u = mt * 4 + nb
                        b = 2 + u % 3
                        bw(b)
                        last = None
                        for fc in range(FC):
                            last = t.matmul(pb[b],
                                            hfmb[:, fc * 1024 + mt * 128: fc * 1024 + (mt + 1) * 128],
                                            X["w2sb"][:, fc * 2048 + nb * 512: fc * 2048 + (nb + 1) * 512],
                                            start=(fc == 0), stop=(fc == FC - 1))
                        self.sig(last, "pe", f"o:{hf}:{e}:{u}", E)
                        bankev[b] = f"oev:{hf}:{e}:{u}"

            for ei, e in enumerate(EORD):
                emit_w1(e)
                emit_w2(e, 0)
            for mt in range(MT):
                self.w(t, E, f"gTe:{hf}:{mt}")
                for nb in range(NB):
                    b = 5 + (mt * 4 + nb) % 2
                    bw(b)
                    mm = t.matmul(pb[b], X["gT"][:, mt * 128:(mt + 1) * 128],
                                  X["b2sb"][:, nb * 512:(nb + 1) * 512],
                                  start=True, stop=True)
                    self.sig(mm, "pe", f"B2:{hf}:{mt}:{nb}", E)
                    bankev[b] = f"b2ev:{hf}:{mt}:{nb}"

    def s_stream(self, sc):
        E = "sc"
        X, pb = self.tn, self.pb
        R = X["rows"]

        for m in range(16):
            self.w(sc, E, f"x0:{m}")
            stg = X["stg0"] if m % 2 == 0 else X["stg1"]
            xin = X["xin0"] if m % 2 == 0 else X["xin1"]
            i = sc.activation(stg[:], xin[:], AF.Square, accum_out=R[:, 1:2])
            self.sig(i, "sc", f"sq0:{m}", E)
            self.w(sc, E, f"var0:{m}")
            sc.activation(R[:, 4:5], R[:, 3:4], AF.Ln, bias=X["epsc"][:])
            i = sc.activation(R[:, 5:6], R[:, 4:5], AF.Exp, scale=-0.5)
            self.sig(i, "sc", f"rstd0:{m}", E)

        for g in range(NG):
            for xn in ("q", "k"):
                src = X["Q"] if xn == "q" else X["K"]
                stg = X["stg0"] if xn == "q" else X["stg1"]
                for mi in range(4):
                    self.w(sc, E, f"rope:{g}:{xn}:3" if mi == 0 else None)
                    self.w(sc, E, f"nrm:{g}:{xn}:{mi - 1}" if mi > 0 else None)
                    i = sc.activation(stg[:], src[:, mi * 2048:(mi + 1) * 2048], AF.Square)
                    self.sig(i, "sc", f"qsq:{g}:{xn}:{mi}", E)
            self.w(sc, E, f"crow0:{g}")
            sc.activation(X["crow"][0:1, 8 * g: 8 * g + 8],
                          X["crow"][0:1, 8 * g: 8 * g + 8], AF.Ln)
            i = sc.activation(X["crow"][0:1, 8 * g: 8 * g + 8],
                              X["crow"][0:1, 8 * g: 8 * g + 8], AF.Exp, scale=0.5)
            self.sig(i, "sc", f"crowS:{g}", E)
            self.w(sc, E, f"expbv:{g}")
            for qb in range(NB):
                for pr in range(4):
                    kl = kts_for(qb)
                    for i_, (kt, off) in enumerate(kl):
                        gi = GI[(g, qb, pr, i_)]
                        self.w(sc, E, f"S:{g}:{qb}:{pr}:{i_}")
                        if gi >= 3:
                            pg, pq, pp, pi = GI_AV[gi - 3]
                            self.w(sc, E, f"AVk:{pg}:{pq}:{pp}:{pi}")
                        j = i_ % 2
                        D = self.pbig[0][:, j * 1024:(j + 1) * 1024]
                        cP = X["expb"][:, 4 * g + pr: 4 * g + pr + 1]
                        if off == 0:
                            i2 = sc.activation(self.PT(gi)[:, 0:1024], D[:, 0:1024],
                                               AF.Exp, bias=cP, scale=0.125)
                        else:
                            sc.activation(self.PT(gi)[:, off:512], D[:, off:512],
                                          AF.Exp, bias=cP, scale=0.125)
                            i2 = sc.activation(self.PT(gi)[:, 512 + off:1024], D[:, 512 + off:1024],
                                               AF.Exp, bias=cP, scale=0.125)
                        self.sig(i2, "sc", f"exp:{g}:{qb}:{pr}:{i_}", E)

        for hf in range(HF):
            for mt in range(MT):
                self.w(sc, E, f"x1r:{hf}:{mt}")
                if mt > 0:
                    self.w(sc, E, f"tr2:{hf}:{mt - 1}:3")
                elif hf == 1:
                    self.w(sc, E, f"tr2:0:{MT - 1}:3")
                i = sc.activation(X["tmpf"][:], X["x1h"][:, mt * 2048:(mt + 1) * 2048],
                                  AF.Square, accum_out=R[:, 9:10])
                self.sig(i, "sc", f"sq2:{hf}:{mt}", E)
                self.w(sc, E, f"var2:{hf}:{mt}")
                sc.activation(R[:, 12:13], R[:, 11:12], AF.Ln, bias=X["epsc"][:])
                i = sc.activation(R[:, 13:14], R[:, 12:13], AF.Exp, scale=-0.5)
                self.sig(i, "sc", f"rstd2:{hf}:{mt}", E)
            for mt in range(MT):
                self.w(sc, E, f"sc:{hf}:{mt}")
                if hf + mt > 0:
                    pv = (hf, mt - 1) if mt > 0 else (hf - 1, MT - 1)
                    self.w(sc, E, f"gbf:{pv[0]}:{pv[1]}")
                i = sc.activation(X["scs"][:], self.pbig[1][:, 0:16], AF.Sigmoid)
                self.sig(i, "sc", f"sig:{hf}:{mt}", E)
            for ei, e in enumerate(EORD):
                gei = hf * NE + ei
                if gei >= 1:
                    hf2, ei2 = divmod(gei - 1, NE)
                    self.w(sc, E, f"o:{hf2}:{EORD[ei2]}:31")
                if hf == 0 and ei == 0:
                    self.w(sc, E, "c:p3")
                hfmb = X["hfm"]
                for fc in range(FC):
                    for nb in range(2):
                        u = fc * 2 + nb
                        self.w(sc, E, f"h:{hf}:{e}:{u}")
                        i = sc.activation(hfmb[:, fc * 1024 + nb * 512: fc * 1024 + (nb + 1) * 512],
                                          pb[u % 2], AF.Relu,
                                          bias=X["b1h"][:, e * 11 + fc: e * 11 + fc + 1])
                        self.sig(i, "sc", f"relu:{hf}:{e}:{u}", E)

    def v_stream(self, v):
        E = "ve"
        X, pb = self.tn, self.pb
        R = X["rows"]
        xhat_v = X["xhat"].ap().rearrange("p (c n) -> p c n", c=16)
        v.memset(X["epsc"][:], EPS)
        self.w(v, E, "c:sync")

        for m in range(16):
            self.w(v, E, f"x0:{m}")
            xin = X["xin0"] if m % 2 == 0 else X["xin1"]
            xtf = X["xtf0"] if m % 2 == 0 else X["xtf1"]
            v.reduce_sum(R[:, 0:1], xin[:], axis=AX.X)
            v.drain()
            v.tensor_scalar(R[:, 6:7], R[:, 0:1], 1.0 / D, None, ALU.mult)
            v.tensor_tensor(R[:, 2:3], R[:, 0:1], R[:, 0:1], ALU.mult)
            self.w(v, E, f"sq0:{m}")
            v.drain()
            v.tensor_scalar(R[:, 7:8], R[:, 2:3], 1.0 / (float(D) * D), None, ALU.mult)
            v.drain()
            i = v.tensor_scalar(R[:, 3:4], R[:, 1:2], 1.0 / D, R[:, 7:8],
                                ALU.mult, ALU.subtract)
            self.sig(i, "ve", f"var0:{m}", E)
            self.w(v, E, f"rstd0:{m}")
            self.w(v, E, f"tr0:{m - 2}:3" if m >= 2 else None)
            i = v.tensor_scalar(xtf[:], xin[:], R[:, 6:7], R[:, 5:6],
                                ALU.subtract, ALU.mult)
            self.sig(i, "ve", f"nrm0:{m}", E)
            for b4 in range(4):
                self.w(v, E, f"tr0:{m}:{b4}")
                src = pb[b4 % 2].rearrange("p (c n) -> p c n", n=128)
                i = v.tensor_copy(xhat_v[:, 4 * b4: 4 * b4 + 4, m * 128:(m + 1) * 128], src)
                self.sig(i, "ve", f"trev0:{m}:{b4}", E)

        ctxw_last = {}
        self.smw_hist = {}
        for g in range(NG):
            self.w(v, E, "c:gp" if g == 0 else None)
            for mi in range(4):
                c = 4 * g + mi
                ct = X["cosT0"] if c % 2 == 0 else X["cosT1"]
                st_ = X["sinT0"] if c % 2 == 0 else X["sinT1"]
                for xi, xn in enumerate(("q", "k")):
                    stg = X["stg0"] if xn == "q" else X["stg1"]
                    dst = X["Q"] if xn == "q" else X["K"]
                    if g > 0 and mi == 0:
                        self.w(v, E, f"nrm:{g - 1}:{xn}:3")
                    for nb in range(NB):
                        self.w(v, E, f"qk:{g}:{xn}:{mi}:{nb}")
                        i = v.tensor_scalar(stg[:, nb * 512:(nb + 1) * 512], pb[nb % 2],
                                            X["qkb"][:, xi * 16 + 4 * g + mi: xi * 16 + 4 * g + mi + 1],
                                            None, ALU.add)
                        self.sig(i, "ve", f"qkevS:{g}:{xn}:{mi}:{nb}", E)
                    v.drain()
                    if xi == 0:
                        self.w(v, E, f"rt:{c}")
                    d = dst[:, mi * 2048:(mi + 1) * 2048]
                    v.tensor_tensor(X["t1b"][:], stg[:], ct[:], ALU.mult)
                    for h2 in range(2):
                        b = 64 * h2
                        v.tensor_tensor(X["t2b"][b:b + 32, :], stg[b + 32:b + 64, :],
                                        st_[b + 32:b + 64, :], ALU.mult)
                        v.tensor_tensor(X["t2b"][b + 32:b + 64, :], stg[b:b + 32, :],
                                        st_[b:b + 32, :], ALU.mult)
                    v.drain()
                    i = v.tensor_tensor(d, X["t1b"][:], X["t2b"][:], ALU.add)
                    self.sig(i, "ve", f"rope:{g}:{xn}:{mi}", E)
            for mt in range(16):
                self.w(v, E, f"vv:{g}:{mt}")
                vt = X["V"][:, mt * 520:(mt + 1) * 520].rearrange("p (h c) -> p h c", c=65)
                i = v.tensor_tensor(vt[:, :, 0:64],
                                    pb[2 + mt % 2].rearrange("p (h c) -> p h c", c=64),
                                    X["bvb"][:, g * 512:(g + 1) * 512].rearrange("p (h c) -> p h c", c=64),
                                    ALU.add)
                v.memset(vt[:, :, 64:65], 1.0)
                self.sig(i, "ve", f"vev:{g}:{mt}", E)
            for xi, xn in enumerate(("q", "k")):
                for mi in range(4):
                    self.w(v, E, f"nmrow:{g - 1}" if (g > 0 and xi == 0 and mi == 0) else None)
                    self.w(v, E, f"nrm:{g}:{xn}:{mi}")
                    i = v.reduce_max(R[0:2, 10 + 4 * xi + mi: 11 + 4 * xi + mi],
                                     self.pbig[1][0:2, 0:2048], axis=AX.X)
                    self.sig(i, "ve", f"nmax:{g}:{xn}:{mi}", E)
                    self.sig(i, "ve", f"nmaxF:{g}:{xn}:{mi}", E) if False else None
                    if self.pass0:
                        self.reg[f"nmaxF:{g}:{xn}:{mi}"] = self.reg[f"nmax:{g}:{xn}:{mi}"]
            self.w(v, E, f"nmrow:{g}")
            i = v.tensor_tensor(X["crow"][0:1, 8 * g:8 * g + 8], X["qrow"][0:1, 8 * g:8 * g + 8],
                                X["krow"][0:1, 8 * g:8 * g + 8], ALU.mult)
            self.sig(i, "ve", f"crow0:{g}", E)
            self.w(v, E, f"crowS:{g}")
            v.tensor_scalar(X["crow"][0:1, 8 * g:8 * g + 8], X["crow"][0:1, 8 * g:8 * g + 8],
                            -0.125 * 1.002, None, ALU.mult)
            v.drain()
            i = v.tensor_tensor(X["crow"][0:1, 8 * g:8 * g + 4], X["crow"][0:1, 8 * g:8 * g + 4],
                                X["crow"][0:1, 8 * g + 4:8 * g + 8], ALU.min)
            self.sig(i, "ve", f"crow1:{g}", E)
            self.w(v, E, f"expbm:{g}")
            i = v.tensor_copy(X["expb"][:, 4 * g: 4 * g + 4], self.pbig[1][:, 0:4])
            self.sig(i, "ve", f"expbv:{g}", E)
            for qb in range(NB):
                for pr in range(4):
                    kl = kts_for(qb)
                    n = len(kl)
                    o2_ = 4 + 2 * (pr % 2)
                    self.w(v, E, f"AVk:{g}:{qb}:{pr}:{n - 1}")
                    slot = pr % 2
                    self.w(v, E, ctxw_last.get(slot))
                    self.w(v, E, self.smw_hist.get(slot))
                    self.smw_hist[slot] = f"smw:{g}:{qb}:{pr}"
                    cstg = X["ctstg0"] if slot == 0 else X["ctstg1"]
                    smA = X["smA"] if slot == 0 else X["smA1"]
                    smB = X["smB"] if slot == 0 else X["smB1"]
                    v.tensor_copy(cstg[0:64, :], pb[o2_][0:64, :])
                    v.tensor_copy(cstg[64:128, :], pb[o2_ + 1][0:64, :])
                    v.reciprocal(smA[:], pb[o2_][64:65, :])
                    i = v.reciprocal(smB[:], pb[o2_ + 1][64:65, :])
                    self.sig(i, "ve", f"ctx:{g}:{qb}:{pr}", E)
                    ctxw_last[slot] = f"ctxw:{g}:{qb}:{pr}"

        for kc in range(KC):
            self.w(v, E, f"ctxr:{KC - 1}" if kc == 0 else None)
            self.w(v, E, f"rbd:{kc}")
            i = v.tensor_tensor(X["ctx"][:, kc * 2048:(kc + 1) * 2048],
                                X["ctx"][:, kc * 2048:(kc + 1) * 2048],
                                X["rbsb"][:], ALU.mult)
            self.sig(i, "ve", f"rbn:{kc}", E)
        for m in range(16):
            self.w(v, E, f"x2:{m}")
            self.w(v, E, f"x1w:{m - 2}" if m >= 2 else None)
            self.w(v, E, "wobB" if m == 0 else None)
            x1 = X["x1sb0"] if m % 2 == 0 else X["x1sb1"]
            xr = X["xr0"] if m % 2 == 0 else X["xr1"]
            for nb in range(NB):
                self.w(v, E, "p2mmdone" if (m == 15 and nb == 3) else f"p2m:{m}:{nb}")
                i = v.tensor_tensor(x1[:, nb * 512:(nb + 1) * 512], pb[nb],
                                    X["wobB"][:, nb * 512:(nb + 1) * 512], ALU.add)
                self.sig(i, "ve", f"p2e1:{m}:{nb}", E)
            last = None
            for nb in range(NB):
                last = v.tensor_tensor(x1[:, nb * 512:(nb + 1) * 512],
                                       x1[:, nb * 512:(nb + 1) * 512],
                                       xr[:, nb * 512:(nb + 1) * 512], ALU.add)
            self.sig(last, "ve", f"p2e2:{m}", E)

        for hf in range(HF):
            xh2_v = X["xh2"].ap().rearrange("p (c n) -> p c n", c=16)
            for mt in range(MT):
                self.w(v, E, f"x1r:{hf}:{mt}")
                x1 = X["x1h"][:, mt * 2048:(mt + 1) * 2048]
                v.reduce_sum(R[:, 8:9], x1, axis=AX.X)
                v.drain()
                v.tensor_scalar(R[:, 14:15], R[:, 8:9], 1.0 / D, None, ALU.mult)
                v.tensor_tensor(R[:, 10:11], R[:, 8:9], R[:, 8:9], ALU.mult)
                self.w(v, E, f"sq2:{hf}:{mt}")
                v.drain()
                v.tensor_scalar(R[:, 15:16], R[:, 10:11], 1.0 / (float(D) * D), None, ALU.mult)
                v.drain()
                i = v.tensor_scalar(R[:, 11:12], R[:, 9:10], 1.0 / D, R[:, 15:16],
                                    ALU.mult, ALU.subtract)
                self.sig(i, "ve", f"var2:{hf}:{mt}", E)
                self.w(v, E, f"rstd2:{hf}:{mt}")
                i = v.tensor_scalar(X["tmpf"][:], x1, R[:, 14:15], R[:, 13:14],
                                    ALU.subtract, ALU.mult)
                self.sig(i, "ve", f"ln2a:{hf}:{mt}", E)
                v.drain()
                i = v.tensor_tensor(x1, x1, X["tmpf"][:], ALU.add)
                self.sig(i, "ve", f"ln2b:{hf}:{mt}", E)
                for b4 in range(4):
                    self.w(v, E, f"tr2:{hf}:{mt}:{b4}")
                    src = pb[5 + b4 % 2].rearrange("p (c n) -> p c n", n=128)
                    i = v.tensor_copy(xh2_v[:, 4 * b4:4 * b4 + 4, mt * 128:(mt + 1) * 128], src)
                    self.sig(i, "ve", f"trev2:{hf}:{mt}:{b4}", E)
            for mt in range(MT):
                if mt == 0:
                    self.w(v, E, "c:p3" if hf == 0 else None)
                self.w(v, E, f"sig:{hf}:{mt}")
                v.tensor_tensor(X["scs"][:], X["scs"][:], X["cbb"][:], ALU.add)
                v.drain()
                v.tensor_copy(X["gt16"][:], X["scs"][:])
                v.drain()
                for r in range(3):
                    v.reduce_max(R[:, 16:17], X["scs"][:], axis=AX.X)
                    v.drain()
                    v.tensor_scalar(X["msk"][:], X["scs"][:], R[:, 16:17], None, ALU.is_ge)
                    v.drain()
                    v.scalar_tensor_tensor(X["scs"][:], X["msk"][:], NEG, X["scs"][:],
                                           ALU.mult, ALU.add)
                    v.drain()
                v.reduce_max(R[:, 20:21], X["scs"][:], axis=AX.X)
                v.drain()
                v.tensor_scalar(X["msk"][:], X["gt16"][:], R[:, 20:21], None, ALU.is_ge)
                v.drain()
                v.tensor_tensor(X["gt16"][:], X["gt16"][:], X["msk"][:], ALU.mult)
                v.drain()
                v.reduce_sum(R[:, 21:22], X["gt16"][:], axis=AX.X)
                v.drain()
                v.tensor_scalar(R[:, 21:22], R[:, 21:22], 1e-8, None, ALU.max)
                v.drain()
                v.reciprocal(R[:, 22:23], R[:, 21:22])
                v.drain()
                i = v.tensor_scalar(X["g_all"][:, mt * 16:(mt + 1) * 16], X["gt16"][:],
                                    R[:, 22:23], None, ALU.mult)
                self.sig(i, "ve", f"gbf:{hf}:{mt}", E)
                self.w(v, E, f"gT:{hf}:{mt}")
                i = v.tensor_copy(X["gT"][0:16, mt * 128:(mt + 1) * 128], pb[7][0:16, 0:128])
                self.sig(i, "ve", f"gTe:{hf}:{mt}", E)
            for ei, e in enumerate(EORD):
                for mt in range(MT):
                    for nb in range(NB):
                        u = mt * 4 + nb
                        self.w(v, E, f"o:{hf}:{e}:{u}")
                        bank = pb[2 + u % 3]
                        dst = X["x1h"][:, mt * 2048 + nb * 512: mt * 2048 + (nb + 1) * 512]
                        if e < NSH:
                            i = v.tensor_tensor(dst, bank[:], dst, ALU.add)
                        else:
                            gcol = X["g_all"][:, mt * 16 + (e - NSH): mt * 16 + (e - NSH) + 1]
                            i = v.scalar_tensor_tensor(dst, bank[:], gcol, dst,
                                                       ALU.mult, ALU.add)
                        self.sig(i, "ve", f"oev:{hf}:{e}:{u}", E)
            for mt in range(MT):
                for nb in range(NB):
                    self.w(v, E, f"B2:{hf}:{mt}:{nb}")
                    bank = pb[5 + (mt * 4 + nb) % 2]
                    dst = X["x1h"][:, mt * 2048 + nb * 512: mt * 2048 + (nb + 1) * 512]
                    i = v.tensor_tensor(dst, bank[:], dst, ALU.add)
                    self.sig(i, "ve", f"b2ev:{hf}:{mt}:{nb}", E)

    def sy_stream(self, sy):
        E = "sy"
        X, din = self.tn, self.din
        for nm, dst in (("iden", "idenf"), ("qkb", "qkb"), ("wvb", "bvb"), ("onesr", "onesf")):
            i = sy.dma_start(out=X[dst][:], in_=din[nm])
            self.sig(i, "dx", None, E)
        self.mark("c:sync", "dx")
        for m in range(16):
            if m >= 2:
                self.w(sy, E, f"nrm0:{m - 2}")
                self.w(sy, E, f"sq0:{m - 2}")
            xin = X["xin0"] if m % 2 == 0 else X["xin1"]
            i = sy.dma_start(out=xin[:], in_=din["x"][m * 128:(m + 1) * 128, :])
            self.sig(i, "dx", f"x0:{m}", E)
        if DBG:
            self.w(sy, E, "trev0:15:3")
            self.sig(sy.dma_start(out=self.dbg["d_xhat"], in_=X["xhat"][:]), "dx", None, E)
        for g in range(NG):
            if DBG and g == 1:
                self.w(sy, E, "rope:0:q:3")
                self.w(sy, E, "rope:0:k:3")
                self.w(sy, E, "vev:0:15")
                for nm, src in (("d_q0", "Q"), ("d_k0", "K"), ("d_v0", "V")):
                    self.sig(sy.dma_start(out=self.dbg[nm], in_=X[src][:]), "dx", None, E)
                self.w(sy, E, "expbv:0")
                self.sig(sy.dma_start(out=self.dbg["d_expb"], in_=X["expb"][:]), "dx", None, E)
            for xi, xn in enumerate(("q", "k")):
                for mi in range(4):
                    self.w(sy, E, f"nmaxF:{g}:{xn}:{mi}")
            for xi, xn in enumerate(("q", "k")):
                row = X["qrow"] if xn == "q" else X["krow"]
                for sel in range(2):
                    i = sy.dma_start(out=row[0:1, 8 * g + 4 * sel: 8 * g + 4 * sel + 4],
                                     in_=X["rows"][sel:sel + 1, 10 + 4 * xi: 14 + 4 * xi])
                    self.sig(i, "dx", None, E)
            self.mark(f"nmrow:{g}", "dx")
            for qb in range(NB):
                for pr in range(4):
                    self.w(sy, E, f"ctx:{g}:{qb}:{pr}")
                    cstg = X["ctstg0"] if pr % 2 == 0 else X["ctstg1"]
                    hA = 8 * g + 2 * pr
                    i = sy.dma_start(out=self.ctxs[hA * 64: hA * 64 + 128,
                                                   qb * 512:(qb + 1) * 512],
                                     in_=cstg[:])
                    self.sig(i, "dx", f"ctxw:{g}:{qb}:{pr}", E)
                    smA = X["smA"] if pr % 2 == 0 else X["smA1"]
                    smB = X["smB"] if pr % 2 == 0 else X["smB1"]
                    i = sy.dma_start(out=self.rcps[hA: hA + 1, qb * 512:(qb + 1) * 512],
                                     in_=smA[:])
                    self.sig(i, "dx", None, E)
                    i = sy.dma_start(out=self.rcps[hA + 1: hA + 2, qb * 512:(qb + 1) * 512],
                                     in_=smB[:])
                    self.sig(i, "dx", None, E)
                    self.mark(f"smw:{g}:{qb}:{pr}", "dx")
        sy.wait_ge(self.sems["dx"], self.cnt["dx"])
        self.trace.append((E, "w", "dx", self.cnt["dx"]))
        self.w(sy, E, "AVk:3:3:3:15")
        self.w(sy, E, "ctx:3:3:3")
        self.w(sy, E, "woc:3")
        for kc in range(KC):
            i = sy.dma_start(out=X["wosb"][:, kc * 2048:(kc + 1) * 2048], in_=self.woc[kc])
            self.sig(i, "dx", f"wo:{kc}", E)
        if DBG:
            self.sig(sy.dma_start(out=self.dbg["d_rcps"], in_=self.rcps), "dx", None, E)
        for kc in range(KC):
            i = sy.dma_start(out=X["ctx"][:, kc * 2048:(kc + 1) * 2048],
                             in_=self.ctxs[kc * 128:(kc + 1) * 128, :])
            self.sig(i, "dz", f"ctxr:{kc}" if kc == KC - 1 else None, E)
        for kc in range(KC):
            self.w(sy, E, f"rbn:{kc - 1}" if kc > 0 else None)
            src = bass.AP(self.rcps.tensor, 2 * kc * T,
                          [[T, 2], [0, 64], [1, T]])
            i = sy.dma_start(out=X["rbsb"][:], in_=src)
            self.sig(i, "dy", f"rbd:{kc}", E)
        if DBG:
            self.w(sy, E, "rbn:15")
            self.sig(sy.dma_start(out=self.dbg["d_ctxn"], in_=X["ctx"][:, 0:2048]), "dx", None, E)
        for m in range(16):
            self.w(sy, E, f"p2e2:{m - 2}" if m >= 2 else None)
            xr = X["xr0"] if m % 2 == 0 else X["xr1"]
            i = sy.dma_start(out=xr[:], in_=din["x"][m * 128:(m + 1) * 128, :])
            self.sig(i, "dx", f"x2:{m}", E)
            self.w(sy, E, f"p2e2:{m}")
            x1 = X["x1sb0"] if m % 2 == 0 else X["x1sb1"]
            i = sy.dma_start(out=self.x1s[m * 128:(m + 1) * 128, :], in_=x1[:])
            self.sig(i, "dx", f"x1w:{m}", E)
        for hf in range(HF):
            for mt in range(MT):
                mg = hf * MT + mt
                if hf == 0 and mt == 0:
                    self.w(sy, E, "p2mmdone")
                    self.w(sy, E, "p2e2:15")
                if hf == 1 and mt == 0:
                    self.w(sy, E, f"outw:0:{MT - 1}")
                    sy.wait_ge(self.sems["dx"], self.cnt["dx"])
                    self.trace.append((E, "w", "dx", self.cnt["dx"]))
                self.w(sy, E, f"x1w:{mg}")
                i = sy.dma_start(out=X["x1h"][:, mt * 2048:(mt + 1) * 2048],
                                 in_=self.x1s[mg * 128:(mg + 1) * 128, :])
                self.sig(i, "dx", f"x1r:{hf}:{mt}", E)
            if DBG and hf == 0:
                self.w(sy, E, f"trev2:0:{MT - 1}:3")
                self.sig(sy.dma_start(out=self.dbg["d_xh2"], in_=X["xh2"][:]), "dx", None, E)
                self.w(sy, E, f"gTe:0:{MT - 1}")
                self.sig(sy.dma_start(out=self.dbg["d_gall"], in_=X["g_all"][:]), "dx", None, E)
                self.w(sy, E, "relu:0:0:21")
                self.sig(sy.dma_start(out=self.dbg["d_hfm"], in_=X["hfm"][:]), "dx", None, E)
            for mt in range(MT):
                self.w(sy, E, f"b2ev:{hf}:{mt}:3")
                i = sy.dma_start(out=self.out[(hf * MT + mt) * 128:(hf * MT + mt + 1) * 128, :],
                                 in_=X["x1h"][:, mt * 2048:(mt + 1) * 2048])
                self.sig(i, "dx", f"outw:{hf}:{mt}", E)
        sy.wait_ge(self.sems["dx"], self.cnt["dx"])
        self.trace.append((E, "w", "dx", self.cnt["dx"]))

    def g_stream(self, g):
        E = "gp"
        X, din = self.tn, self.din
        for nm, dst in (("iden", "idenb"), ("tri", "trib"),
                        ("onesr", "onesb"), ("ones2", "ones2b")):
            i = g.dma_start(out=X[dst][:], in_=din[nm])
            self.sig(i, "dw", None, E)
        self.mark("c:gp", "dw")
        slotev = {}
        rtev = {}
        for gr in range(NG):
            for mi in range(4):
                c = 4 * gr + mi
                self.w(g, E, rtev.get(c % 2))
                i = g.dma_start(out=(X["cosT0"] if c % 2 == 0 else X["cosT1"])[:],
                                in_=din["ropeC"][c])
                self.sig(i, "dw", None, E)
                i = g.dma_start(out=(X["sinT0"] if c % 2 == 0 else X["sinT1"])[:],
                                in_=din["ropeS"][c])
                self.sig(i, "dw", f"rt:{c}", E)
                rtev[c % 2] = f"rope:{gr}:k:{mi}"
                for xi, xn in enumerate(("q", "k")):
                    sl = (mi * 2 + xi) % 2
                    self.w(g, E, slotev.get(sl))
                    i = g.dma_start(out=(X["wqk0"] if sl == 0 else X["wqk1"])[:],
                                    in_=din["w" + xn][4 * gr + mi])
                    self.sig(i, "dw", f"w{xn}:{gr}:{mi}", E)
                    slotev[sl] = f"qk:{gr}:{xn}:{mi}:3"
            if gr > 0:
                self.w(g, E, f"vv:{gr - 1}:15")
            else:
                self.w(g, E, "nrm0:15")
                self.w(g, E, "sq0:15")
            for kc in range(KC):
                i = g.dma_start(out=X["wvg"][:, kc * 512:(kc + 1) * 512],
                                in_=din["wv"][kc][:, gr * 512:(gr + 1) * 512])
                self.sig(i, "dw", None, E)
            self.mark(f"wv:{gr}", "dw")
            for kc in range(4 * gr, 4 * gr + 4):
                i = g.dma_start(out=self.woc[kc], in_=din["wo"][kc])
                self.sig(i, "dw", None, E)
            self.mark(f"woc:{gr}", "dw")
        self.w(g, E, "AVk:3:3:3:15")
        self.w(g, E, "ctx:3:3:3")
        i = g.dma_start(out=X["wobB"][:], in_=din["wob"])
        self.sig(i, "dw", "wobB", E)
        for nm, dst in (("b2", "b2sb"), ("cent", "centT"), ("b1", "b1h"), ("cbb", "cbb")):
            i = g.dma_start(out=X[dst][:], in_=din[nm])
            self.sig(i, "dw", None, E)
        i = g.dma_start(out=X["gT"][16:18, :], in_=din["onesg"])
        self.sig(i, "dw", None, E)
        self.mark("c:p3", "dw")
        st_local = {"first": True, "prev_e": None}
        w1slot = {}
        for hf in range(HF):
            def w1_block(e, fc):
                if st_local["first"]:
                    self.w(g, E, "p2mmdone")
                    self.w(g, E, "p2e2:15")
                    self.w(g, E, "x1w:15")
                    st_local["first"] = False
                self.w(g, E, w1slot.get(fc % 2))
                i = g.dma_start(out=(X["w1d0"] if fc % 2 == 0 else X["w1d1"])[:],
                                in_=din["w1"][e][fc])
                self.sig(i, "dw", f"w1:{hf}:{e}:{fc}", E)
                w1slot[fc % 2] = f"h:{hf}:{e}:{fc * 2 + 1}"

            def load_expert(e):
                w1_block(e, 0)
                w1_block(e, 1)
                if st_local["prev_e"] is not None:
                    pv = st_local["prev_e"]
                    self.w(g, E, f"o:{pv[0]}:{pv[1]}:31")
                w2j = 0
                for fc in range(2, FC):
                    for _ in range(2):
                        if w2j < FC:
                            i = g.dma_start(out=X["w2sb"][:, w2j * 2048:(w2j + 1) * 2048],
                                            in_=din["w2"][e][w2j])
                            self.sig(i, "dw", None, E)
                            w2j += 1
                    w1_block(e, fc)
                while w2j < FC:
                    i = g.dma_start(out=X["w2sb"][:, w2j * 2048:(w2j + 1) * 2048],
                                    in_=din["w2"][e][w2j])
                    self.sig(i, "dw", None, E)
                    w2j += 1
                self.mark(f"w2:{hf}:{e}", "dw")
                st_local["prev_e"] = (hf, e)

            for ei, e in enumerate(EORD):
                load_expert(e)


def check_deadlock(trace):
    streams = {}
    for ent in trace:
        streams.setdefault(ent[0], []).append(ent)
    ptrs = {k: 0 for k in streams}
    sems = {}
    progress = True
    while progress:
        progress = False
        for k, lst in streams.items():
            while ptrs[k] < len(lst):
                eng, kind, sem, val = lst[ptrs[k]]
                if kind == "s":
                    sems[sem] = max(sems.get(sem, 0), val)
                    ptrs[k] += 1
                    progress = True
                else:
                    if sems.get(sem, 0) >= val:
                        ptrs[k] += 1
                        progress = True
                    else:
                        break
    stuck = {k: (ptrs[k], len(streams[k])) for k in streams if ptrs[k] < len(streams[k])}
    if stuck:
        detail = {k: streams[k][v[0]] for k, v in stuck.items()}
        raise RuntimeError(f"deadlock: {stuck} heads={detail}")


_CACHE = {}


def _get_nc():
    if "nc" not in _CACHE:
        b0 = Builder(None)
        b0.build()
        b1 = Builder(b0.reg)
        nc = b1.build()
        check_deadlock(b1.trace)
        _CACHE["nc"] = nc
    return _CACHE["nc"]


def kernel(**inputs):
    hp = host_prepare(inputs)
    x = np.asarray(inputs["L0_attn_x"], np.float32)
    nc = _get_nc()
    in_maps = []
    for c in range(NCORE):
        m = dict(hp)
        m["x"] = _t(x[c])
        in_maps.append(m)
    res = run_bass_kernel_spmd(nc, in_maps, core_ids=list(range(NCORE)))
    return np.stack([np.asarray(res.results[c]["out"]) for c in range(NCORE)], axis=0)
